# version 11
# baseline (speedup 1.0000x reference)
import numpy as np
import ml_dtypes
import concourse.bass as bass
import concourse.mybir as mybir

F32 = mybir.dt.float32
BF16 = mybir.dt.bfloat16
AF = mybir.ActivationFunctionType
ALU = mybir.AluOpType
AXX = mybir.AxisListType.X

T = 2048
D = 2048
NQ, NKV, HD, KV = 16, 4, 128, 512
F = 1024
NE, NSH = 16, 2
DT = D // 128
TTN = T // 128
TH = T // 2
TTH = TH // 128
SCALE = float(1.0 / np.sqrt(128.0))
EPS = 1e-5
NEG = -1e30

ENGINES = ("tensor", "vector", "scalar", "gpsimd", "sync")
QUEUES = {"sync": "q_sync", "gpsimd": "q_gpsimd", "scalar": "q_scalar"}
NPAR = 4
QKEYS = tuple(f"{q}_{p}" for q in QUEUES.values() for p in range(NPAR))
ALL_KEYS = ENGINES + QKEYS
PAGE = 4096


class _Op:
    __slots__ = ("engine", "fn", "idx", "deps", "inc_needed", "dma", "clock",
                 "tag", "qidx")

    def __init__(self, engine, fn, idx, dma, tag):
        self.engine, self.fn, self.idx, self.dma, self.tag = \
            engine, fn, idx, dma, tag
        self.deps = []
        self.inc_needed = False
        self.clock = None


class Sched:
    def __init__(self, nc, same_engine_sync=True):
        self.nc = nc
        self.same_engine_sync = same_engine_sync
        self.ops = {e: [] for e in ENGINES}
        self.dma_counts = {q: 0 for q in QUEUES.values()}
        self.dma_issue = {q: [] for q in QUEUES.values()}
        self._frontier = {}
        self._pages = {}
        self._addr_cache = {}

    def _tensor_info(self, name):
        info = self._addr_cache.get(name)
        if info is not None:
            return info
        for alloc in self.nc.m.functions[0].allocations:
            mls = getattr(alloc, "memorylocations", None)
            if not mls:
                continue
            for ml in mls:
                if ml.name not in self._addr_cache:
                    self._addr_cache[ml.name] = (str(ml.type), ml.addr, ml.bank)
        return self._addr_cache[name]

    def _extent(self, ap):
        t = ap.tensor
        typ, addr, bank = self._tensor_info(t.name)
        dsz = mybir.dt.size(ap.dtype)
        aps = list(ap.ap)
        off = ap.offset
        lo = hi = 0
        if typ in ("SB", "PSUM"):
            pstride = aps[0][0]
            base_p = ap.base_partition()
            pcnt = aps[0][1]
            for step, cnt in aps[1:]:
                if cnt <= 1:
                    continue
                d = step * (cnt - 1)
                if d >= 0:
                    hi += d
                else:
                    lo += d
            elem0 = off - base_p * pstride
            if typ == "SB":
                return ("SBUF", base_p, base_p + pcnt,
                        addr + (elem0 + lo) * dsz,
                        addr + (elem0 + hi) * dsz + dsz)
            b_lo = bank * 2048 + (elem0 + lo) * dsz
            b_hi = bank * 2048 + (elem0 + hi) * dsz + dsz
            return ("PSUM", 0, 128, (b_lo // 2048) * 2048,
                    ((b_hi + 2047) // 2048) * 2048)
        for step, cnt in aps:
            if cnt <= 1:
                continue
            d = step * (cnt - 1)
            if d >= 0:
                hi += d
            else:
                lo += d
        return ("DRAM:" + t.name, 0, 128, (off + lo) * dsz,
                (off + hi) * dsz + dsz)

    def op(self, engine, fn, reads=(), writes=(), tag=""):
        return self._rec(engine, fn, reads, writes, None, tag)

    def dma(self, engine, fn, reads=(), writes=(), tag=""):
        return self._rec(engine, fn, reads, writes, QUEUES[engine], tag)

    def _rec(self, engine, fn, reads, writes, dma, tag):
        ops = self.ops[engine]
        o = _Op(engine, fn, len(ops), dma, tag)
        if ops:
            prev = ops[-1]
            o.clock = dict(prev.clock)
            if engine == "tensor":
                o.clock[engine] = prev.idx + 1
        else:
            o.clock = {k: 0 for k in ALL_KEYS}
        deps = {}

        def add(owner):
            key, idx = owner
            cnt = idx + 1
            if cnt > deps.get(key, 0):
                deps[key] = cnt

        rex = [self._extent(ap) for ap in reads]
        wex = [self._extent(ap) for ap in writes]
        for ext in rex:
            for ent in self._conflicts(ext):
                if ent[0] is not None:
                    add(ent[0])
        for ext in wex:
            for ent in self._conflicts(ext):
                if ent[0] is not None:
                    add(ent[0])
                for r in ent[1]:
                    add(r)

        if dma is not None:
            j = self.dma_counts[dma]
            if j >= NPAR:
                deps_key = f"{dma}_{j % NPAR}"
                if deps.get(deps_key, 0) < j // NPAR:
                    deps[deps_key] = j // NPAR

        final = []
        for key, cnt in deps.items():
            if o.clock.get(key, 0) >= cnt:
                continue
            final.append((key, cnt))
        for key, cnt in final:
            if key in ENGINES:
                prod = self.ops[key][cnt - 1]
                assert prod.dma is None, f"engine dep onto dma op {prod.tag}"
                self._merge(o.clock, prod.clock)
                if o.clock[key] < cnt:
                    o.clock[key] = cnt
                prod.inc_needed = True
            else:
                qname, p = key.rsplit("_", 1)
                prod = self.dma_issue[qname][(cnt - 1) * NPAR + int(p)]
                self._merge(o.clock, prod.clock)
                if o.clock[key] < cnt:
                    o.clock[key] = cnt
        o.deps = final

        if dma is None:
            owner = (engine, o.idx)
        else:
            j = self.dma_counts[dma]
            owner = (f"{dma}_{j % NPAR}", j // NPAR)
        for ext in rex:
            self._register(ext, owner, False)
        for ext in wex:
            self._register(ext, owner, True)
        if dma is not None:
            self.dma_counts[dma] += 1
            self.dma_issue[dma].append(o)
            o.qidx = self.dma_counts[dma] - 1
        ops.append(o)
        return o

    def _conflicts(self, ext):
        space = ext[0]
        iv = ext[1:]
        fr = self._frontier.get(space)
        if fr is None:
            return ()
        out = []
        pages = self._pages[space]
        seen = set()
        for pg in range(iv[2] // PAGE, (iv[3] - 1) // PAGE + 1):
            for other in pages.get(pg, ()):
                if other in seen:
                    continue
                seen.add(other)
                if self._overlap(iv, other):
                    e = fr.get(other)
                    if e is not None:
                        out.append(e)
        return out

    @staticmethod
    def _overlap(a, b):
        return not (a[1] <= b[0] or b[1] <= a[0] or a[3] <= b[2] or b[3] <= a[2])

    @staticmethod
    def _covers(a, b):
        return a[0] <= b[0] and a[1] >= b[1] and a[2] <= b[2] and a[3] >= b[3]

    def _register(self, ext, owner, is_write):
        space = ext[0]
        iv = ext[1:]
        fr = self._frontier.setdefault(space, {})
        pages = self._pages.setdefault(space, {})
        if is_write:
            drop = set()
            for pg in range(iv[2] // PAGE, (iv[3] - 1) // PAGE + 1):
                for other in pages.get(pg, ()):
                    if other != iv and self._covers(iv, other):
                        drop.add(other)
            if drop:
                for other in drop:
                    fr.pop(other, None)
                for pg in list(pages):
                    pages[pg] = [x for x in pages[pg] if x not in drop]
            fr[iv] = (owner, [])
        else:
            e = fr.get(iv)
            if e is None:
                fr[iv] = (None, [owner])
            else:
                e[1].append(owner)
        for pg in range(iv[2] // PAGE, (iv[3] - 1) // PAGE + 1):
            lst = pages.setdefault(pg, [])
            if iv not in lst:
                lst.append(iv)

    @staticmethod
    def _merge(clock, other):
        for k, v in other.items():
            if clock.get(k, 0) < v:
                clock[k] = v

    def emit(self):
        nc = self.nc
        sems = {}
        for key in ALL_KEYS:
            g = nc.semaphore(f"sem_{key}")
            sems[key] = g.__enter__()
        inc_upto = {}
        for e in ENGINES:
            lst = []
            c = 0
            for o in self.ops[e]:
                if o.inc_needed and o.dma is None:
                    c += 1
                lst.append(c)
            inc_upto[e] = lst

        def body(eng, ename):
            watermark = {k: 0 for k in ALL_KEYS}
            for o in self.ops[ename]:
                for key, cnt in o.deps:
                    if key in ENGINES:
                        if key == ename and key == "tensor":
                            continue
                        val = inc_upto[key][cnt - 1]
                        if val <= watermark[key] or val == 0:
                            continue
                        eng.wait_ge(sems[key], val)
                        watermark[key] = val
                    else:
                        val = 16 * cnt
                        if val <= watermark[key]:
                            continue
                        eng.wait_ge(sems[key], val)
                        watermark[key] = val
                inst = o.fn(eng)
                if o.dma is not None:
                    inst.then_inc(sems[f"{o.dma}_{o.qidx % NPAR}"], 16)
                elif o.inc_needed:
                    inst.then_inc(sems[ename], 1)
            q = QUEUES.get(ename)
            if q is not None and self.dma_counts[q] > 0:
                n = self.dma_counts[q]
                for p in range(NPAR):
                    cnt = (n - p + NPAR - 1) // NPAR
                    if cnt > 0 and watermark[f"{q}_{p}"] < 16 * cnt:
                        eng.wait_ge(sems[f"{q}_{p}"], 16 * cnt)

        with self.nc.Block() as block:
            block.tensor(lambda eng: body(eng, "tensor"))
            block.vector(lambda eng: body(eng, "vector"))
            block.scalar(lambda eng: body(eng, "scalar"))
            block.gpsimd(lambda eng: body(eng, "gpsimd"))
            block.sync(lambda eng: body(eng, "sync"))
        return sems

    def stats(self):
        return {"ops": {e: len(self.ops[e]) for e in ENGINES},
                "deps": sum(len(o.deps) for e in ENGINES for o in self.ops[e]),
                "dmas": dict(self.dma_counts)}


def prep_shared(ins):
    bf = ml_dtypes.bfloat16
    s = {}
    s["wq"] = np.ascontiguousarray(
        ins["L0_attn_Wq"].reshape(16, 128, 16, 128).transpose(0, 3, 2, 1))
    s["wk"] = np.ascontiguousarray(
        ins["L0_attn_Wk"].reshape(4, 128, 16, 128).transpose(0, 3, 2, 1))
    s["wv"] = np.ascontiguousarray(
        ins["L0_attn_Wv"].reshape(512, 16, 128).transpose(2, 1, 0))
    s["wo"] = np.ascontiguousarray(
        ins["L0_attn_Wo"].reshape(4, 512, 16, 128).transpose(0, 3, 2, 1))
    s["bq"] = ins["L0_attn_bq"].reshape(1, D)
    s["bk"] = ins["L0_attn_bk"].reshape(1, KV)
    s["bv"] = ins["L0_attn_bv"].reshape(1, KV)
    s["bo"] = ins["L0_attn_bo"].reshape(1, D)
    s["wa"] = np.ascontiguousarray(
        ins["L0_ffn_wa"].reshape(16, 128, 16).transpose(1, 0, 2))
    def gw(x, ne):
        return np.ascontiguousarray(
            x.reshape(ne, 16, 128, 2, 512).transpose(0, 3, 2, 1, 4))

    def w2w(x, ne):
        return np.ascontiguousarray(
            x.reshape(ne, 2, 4, 128, D).transpose(0, 1, 3, 2, 4))

    s["rg"] = gw(ins["L0_ffn_rg"], NE)
    s["rw1"] = gw(ins["L0_ffn_rw1"], NE)
    s["rw2"] = w2w(ins["L0_ffn_rw2"], NE)
    s["sg"] = gw(ins["L0_ffn_sg"], NSH)
    s["sw1"] = gw(ins["L0_ffn_sw1"], NSH)
    s["sw2"] = w2w(ins["L0_ffn_sw2"], NSH)
    s["rgb"] = np.ascontiguousarray(
        ins["L0_ffn_rgb"].reshape(NE, 8, 128).transpose(0, 2, 1))
    s["rb1"] = np.ascontiguousarray(
        ins["L0_ffn_rb1"].reshape(NE, 8, 128).transpose(0, 2, 1))
    s["rb2"] = np.ascontiguousarray(ins["L0_ffn_rb2"])
    s["sgb"] = np.ascontiguousarray(
        ins["L0_ffn_sgb"].reshape(NSH, 8, 128).transpose(0, 2, 1))
    s["sb1"] = np.ascontiguousarray(
        ins["L0_ffn_sb1"].reshape(NSH, 8, 128).transpose(0, 2, 1))
    s["sb2"] = np.ascontiguousarray(ins["L0_ffn_sb2"])

    j = np.arange(64, dtype=np.float64)
    th = 10000.0 ** (-2.0 * j / 128.0)
    ang = np.arange(1, T + 1, dtype=np.float64)[:, None] * th
    cosT = np.cos(ang).T
    sinT = np.sin(ang).T
    s["cc"] = np.ascontiguousarray(
        np.concatenate([cosT, cosT], axis=0)).astype(bf)
    s["ss"] = np.ascontiguousarray(
        np.concatenate([-sinT, sinT], axis=0)).astype(bf)
    pm = np.zeros((128, 128), dtype=bf)
    for mcol in range(128):
        pm[(mcol + 64) % 128, mcol] = 1.0
    s["perm"] = pm
    s["ident"] = np.eye(128, dtype=bf)
    s["ones"] = np.ones((128, 512), dtype=bf)
    mk = np.zeros((128, 4, 512), dtype=np.float32)
    p = np.arange(128)[:, None]
    jj = np.arange(512)[None, :]
    for mm_ in range(4):
        mk[:, mm_, :] = np.where(mm_ * 128 + p > jj, NEG, 0.0)
    s["masks"] = np.ascontiguousarray(mk.reshape(128, 4 * 512)).astype(bf)
    return s


INPUT_SPECS = [
    ("x", [T, D], F32),
    ("wq", [16, 128, 16, 128], F32),
    ("wk", [4, 128, 16, 128], F32),
    ("wv", [128, 16, 512], F32),
    ("wo", [4, 128, 16, 512], F32),
    ("bq", [1, D], F32),
    ("bk", [1, KV], F32),
    ("bv", [1, KV], F32),
    ("bo", [1, D], F32),
    ("wa", [128, 16, 16], F32),
    ("rg", [NE, 2, 128, 16, 512], F32),
    ("rw1", [NE, 2, 128, 16, 512], F32),
    ("rw2", [NE, 2, 128, 4, D], F32),
    ("sg", [NSH, 2, 128, 16, 512], F32),
    ("sw1", [NSH, 2, 128, 16, 512], F32),
    ("sw2", [NSH, 2, 128, 4, D], F32),
    ("rgb", [NE, 128, 8], F32),
    ("rb1", [NE, 128, 8], F32),
    ("rb2", [NE, D], F32),
    ("sgb", [NSH, 128, 8], F32),
    ("sb1", [NSH, 128, 8], F32),
    ("sb2", [NSH, D], F32),
    ("cc", [128, T], BF16),
    ("ss", [128, T], BF16),
    ("perm", [128, 128], BF16),
    ("ident", [128, 128], BF16),
    ("ones", [128, 512], BF16),
    ("masks", [128, 4 * 512], BF16),
]

DBG_SPECS = {
    "ln1T": ([D, T], BF16), "qT": ([D, T], BF16), "kT": ([KV, T], BF16),
    "v": ([T, KV], BF16), "yT": ([D, T], BF16), "x1": ([T, D], F32),
    "ln2T": ([D, TH], BF16), "aff": ([T, 16], F32), "c": ([T, 16], F32),
    "acc": ([T, D], F32),
}




class Builder:
    def __init__(self, nc, io, out, dbg_t):
        self.nc = nc
        self.io = io
        self.out = out
        self.dbg_t = dbg_t
        self.S = Sched(nc)

    def dma(self, eng, dst, src, tag=""):
        self.S.dma(eng, lambda e, d=dst, s=src: e.dma_start(out=d, in_=s),
                   reads=[src], writes=[dst], tag=tag)

    def mm(self, out_ap, lhsT, rhs, start, stop, tag=""):
        self.S.op("tensor",
                  lambda e, o=out_ap, l=lhsT, r=rhs, st=start, sp=stop:
                  e.matmul(o, l, r, start=st, stop=sp),
                  reads=[lhsT, rhs], writes=[out_ap], tag=tag)

    def transpose(self, out_ap, in_ap, tag=""):
        ident = self.ident
        self.S.op("tensor",
                  lambda e, o=out_ap, i=in_ap, d=ident: e.transpose(o, i, d),
                  reads=[in_ap, ident], writes=[out_ap], tag=tag)

    def act(self, out_ap, in_ap, func, bias=0.0, scale=1.0, tag=""):
        reads = [in_ap]
        if isinstance(bias, bass.AP):
            reads.append(bias)
        if isinstance(scale, bass.AP):
            reads.append(scale)
        self.S.op("scalar",
                  lambda e, o=out_ap, i=in_ap, f=func, b=bias, s=scale:
                  e.activation(o, i, f, bias=b, scale=s),
                  reads=reads, writes=[out_ap], tag=tag)

    def vop(self, fn, reads, writes, tag=""):
        self.S.op("vector", fn, reads=reads, writes=writes, tag=tag)

    def tt(self, out_ap, a, b, opk, tag=""):
        self.vop(lambda e, o=out_ap, x=a, y=b, k=opk:
                 e.tensor_tensor(o, x, y, k), [a, b], [out_ap], tag)

    def ts(self, out_ap, a, s1, s2, op0, op1=ALU.bypass, tag=""):
        reads = [a] + [s for s in (s1, s2) if isinstance(s, bass.AP)]
        self.vop(lambda e, o=out_ap, x=a, u=s1, w=s2, k0=op0, k1=op1:
                 e.tensor_scalar(o, x, u, w, k0, k1), reads, [out_ap], tag)

    def stt(self, out_ap, in0, scalar, in1, op0, op1, tag=""):
        reads = [in0, in1] + ([scalar] if isinstance(scalar, bass.AP) else [])
        self.vop(lambda e, o=out_ap, a=in0, s=scalar, b=in1, k0=op0, k1=op1:
                 e.scalar_tensor_tensor(o, a, s, b, k0, k1),
                 reads, [out_ap], tag)

    def vcopy(self, out_ap, in_ap, tag=""):
        self.vop(lambda e, o=out_ap, i=in_ap: e.tensor_copy(o, i),
                 [in_ap], [out_ap], tag)

    def vrecip(self, out_ap, in_ap, tag=""):
        nc = self.nc

        def fn(e, o=out_ap, i=in_ap):
            if o.dtype == BF16:
                with nc.allow_low_precision("bf16 softmax scale operand"):
                    return e.reciprocal(o, i)
            return e.reciprocal(o, i)

        self.vop(fn, [in_ap], [out_ap], tag)

    def sact(self, out_ap, in_ap, func, scale=1.0, tag=""):
        self.S.op("scalar",
                  lambda e, o=out_ap, i=in_ap, f=func, s=scale:
                  e.activation(o, i, f, scale=s),
                  reads=[in_ap], writes=[out_ap], tag=tag)

    def ln_stats_scalar(self, xa, dump_ps, parts, veps, rstd, negmr,
                        n=D, tag=""):
        nch = n // 512
        for ch in range(nch):
            ia = xa[:, ch * 512:(ch + 1) * 512]
            pa = parts[:, ch:ch + 1]
            dp = dump_ps[ch % 2]
            self.S.op("scalar",
                      lambda e, o=dp, i=ia, a=pa:
                      e.activation(o, i, AF.Square, accum_out=a),
                      reads=[ia], writes=[dp, pa], tag=tag)
        for ch in range(nch):
            ia = xa[:, ch * 512:(ch + 1) * 512]
            pa = parts[:, 4 + ch:5 + ch]
            dp = dump_ps[ch % 2]
            self.S.op("scalar",
                      lambda e, o=dp, i=ia, a=pa:
                      e.activation(o, i, AF.Copy, accum_out=a),
                      reads=[ia], writes=[dp, pa], tag=tag)
        s2 = parts[:, 8:9]
        m = parts[:, 9:10]
        self.vop(lambda e, o=s2, i=parts[:, 0:4]: e.reduce_sum(o, i, AXX),
                 [parts[:, 0:4]], [s2], tag)
        self.vop(lambda e, o=m, i=parts[:, 4:8]: e.reduce_sum(o, i, AXX),
                 [parts[:, 4:8]], [m], tag)
        self.ts(m, m, 1.0 / n, None, ALU.mult, tag=tag)
        self.stt(veps, m, -1.0, m, ALU.mult, ALU.mult, tag=tag)
        self.stt(veps, s2, 1.0 / n, veps, ALU.mult, ALU.add, tag=tag)
        self.ts(veps, veps, EPS, None, ALU.add, tag=tag)
        self.sact(veps, veps, AF.Ln, tag=tag)
        self.sact(rstd, veps, AF.Exp, scale=-0.5, tag=tag)
        self.stt(negmr, m, -1.0, rstd, ALU.mult, ALU.mult, tag=tag)

    def ln_stats(self, xa, bnb, agg, veps, rstd, negmr, n=D, tag=""):
        for ch in range(n // 512):
            ob = bnb[:, ch * 6:(ch + 1) * 6]
            ia = xa[:, ch * 512:(ch + 1) * 512]
            self.vop(lambda e, o=ob, i=ia: e.bn_stats(o, i), [ia], [ob], tag)
        bslice = bnb[:, 0:6 * (n // 512)]
        self.vop(lambda e, o=agg, i=bslice: e.bn_aggr(o, i),
                 [bslice], [agg], tag)
        self.ts(veps, agg[:, 1:2], EPS, None, ALU.add, tag=tag)
        self.sact(veps, veps, AF.Ln, tag=tag)
        self.sact(rstd, veps, AF.Exp, scale=-0.5, tag=tag)
        self.stt(negmr, agg[:, 0:1], -1.0, rstd, ALU.mult, ALU.mult, tag=tag)

    def build(self, stop_after=None):
        nc = self.nc
        io = self.io
        gp, sy = "gpsimd", "sync"

        self.ones_g = nc.sbuf_tensor("s_ones", [128, 512], BF16)
        self.ident_g = nc.sbuf_tensor("s_ident", [128, 128], BF16)
        self.masks_g = nc.sbuf_tensor("s_masks", [128, 4 * 512], BF16,
                                      side="right")
        self.ones = self.ones_g.__enter__().ap()
        self.ident = self.ident_g.__enter__().ap()
        self.masks = self.masks_g.__enter__().ap()
        self.dma(gp, self.ones, io["ones"])
        self.dma(gp, self.ident, io["ident"])
        self.dma(gp, self.masks, io["masks"])

        self.p1()
        if stop_after == "P1":
            self.S.emit()
            return self.S
        self.p2()
        if stop_after == "P2":
            self.S.emit()
            return self.S
        self.p3()
        if stop_after == "P3":
            self.S.emit()
            return self.S
        self.p4()
        self.S.emit()
        return self.S

    def p1(self):
        nc, io, dbg_t = self.nc, self.io, self.dbg_t
        gp, sy = "gpsimd", "sync"
        mm, act, tt, vcopy = self.mm, self.act, self.tt, self.vcopy
        dma, transpose = self.dma, self.transpose
        ones = self.ones

        self.qT_g = nc.sbuf_tensor("qT", [128, NQ * T], BF16)
        self.kT_g = nc.sbuf_tensor("kT", [128, NKV * T], BF16)
        self.vsb_g = nc.sbuf_tensor("vsb", [128, TTN * KV], BF16)
        self.qT = self.qT_g.__enter__().ap()
        self.kT = self.kT_g.__enter__().ap()
        self.vsb = self.vsb_g.__enter__().ap()
        qT, kT, vsb = self.qT, self.kT, self.vsb

        ln1T_g = nc.sbuf_tensor("ln1T", [128, DT * T], BF16)
        ln1T = ln1T_g.__enter__().ap()

        self.bv_g = nc.sbuf_tensor("s_bv", [1, KV], BF16)
        bv = self.bv_g.__enter__().ap()
        dma(gp, bv, io["bv"])
        ln1T3 = ln1T.rearrange("p (j t) -> p j t", j=DT)

        wv_g = nc.sbuf_tensor("wv_sb", [128, 16 * 512], BF16)
        wv_sb = wv_g.__enter__().ap()
        dma(gp, wv_sb, io["wv"].rearrange("p i o -> p (i o)"), tag="wv")

        x_t_g = nc.sbuf_tensor("x_t", [128, 2 * D], F32)
        ln1_t_g = nc.sbuf_tensor("ln1_t", [128, D], BF16)
        bn_g = nc.sbuf_tensor("bn_s", [128, 3 * 32], F32)
        st_g = nc.sbuf_tensor("st_s", [128, 3 * 8], F32)
        tp0 = nc.psum_tensor("tr_ps0", [128, 1024], BF16)
        tp1 = nc.psum_tensor("tr_ps1", [128, 1024], BF16)
        vps = [nc.psum_tensor(f"v_ps{i}", [128, 512], F32) for i in range(2)]
        x_t = x_t_g.__enter__().ap()
        ln1_t = ln1_t_g.__enter__().ap()
        bn = bn_g.__enter__().ap()
        sts = st_g.__enter__().ap()
        trp = [tp0.__enter__().ap(), tp1.__enter__().ap()]
        vpsa = [g.__enter__().ap() for g in vps]
        for ti in range(TTN):
            pb = ti % 2
            xa = x_t[:, pb * D:(pb + 1) * D]
            dma(sy, xa, io["x"][ti * 128:(ti + 1) * 128, :], tag=f"x{ti}")
            bnb = bn[:, pb * 32:pb * 32 + 32]
            agg = sts[:, pb * 8:pb * 8 + 2]
            veps = sts[:, pb * 8 + 2:pb * 8 + 3]
            rstd = sts[:, pb * 8 + 3:pb * 8 + 4]
            negmr = sts[:, pb * 8 + 4:pb * 8 + 5]
            self.ln_stats(xa, bnb, agg, veps, rstd, negmr, tag=f"ln1.{ti}")
            la = ln1_t[:, 0:D]
            act(la, xa, AF.Identity, bias=negmr, scale=rstd, tag=f"ln1a.{ti}")
            for half in range(2):
                pt = trp[half]
                for jj in range(8):
                    j = half * 8 + jj
                    transpose(pt[:, jj * 128:(jj + 1) * 128],
                              la[:, j * 128:(j + 1) * 128], tag=f"tr1.{ti}.{j}")
                dst = ln1T3[:, half * 8:(half + 1) * 8, ti * 128:(ti + 1) * 128]
                src = pt.rearrange("p (j c) -> p j c", j=8)
                vcopy(dst, src, tag=f"trc.{ti}.{half}")
            pp = vpsa[ti % 2]
            for i in range(DT):
                mm(pp, ln1T[:, i * T + ti * 128:i * T + (ti + 1) * 128],
                   wv_sb[:, i * 512:(i + 1) * 512],
                   start=(i == 0), stop=False, tag=f"v{ti}")
            mm(pp, ones[0:1, 0:128], bv[0:1, 0:512], start=False, stop=True,
               tag=f"vb{ti}")
            act(vsb[:, ti * 512:(ti + 1) * 512], pp, AF.Copy, tag=f"ve{ti}")
        for g_ in (st_g, bn_g, ln1_t_g, x_t_g, *reversed(vps), tp1, tp0):
            g_.__exit__(None, None, None)
        wv_g.__exit__(None, None, None)
        if "ln1T" in dbg_t:
            for i in range(DT):
                dma(sy, dbg_t["ln1T"][i * 128:(i + 1) * 128, :],
                    ln1T[:, i * T:(i + 1) * T])

        self.bq_g = nc.sbuf_tensor("s_bq", [1, D], BF16)
        self.bk_g = nc.sbuf_tensor("s_bk", [1, KV], BF16)
        self.cos_g = nc.sbuf_tensor("ccb", [128, T], BF16)
        self.sin_g = nc.sbuf_tensor("ssb", [128, T], BF16)
        self.perm_g = nc.sbuf_tensor("permb", [128, 128], BF16)
        bq = self.bq_g.__enter__().ap()
        bk = self.bk_g.__enter__().ap()
        ccb = self.cos_g.__enter__().ap()
        ssb = self.sin_g.__enter__().ap()
        permb = self.perm_g.__enter__().ap()
        dma(gp, bq, io["bq"])
        dma(gp, bk, io["bk"])
        dma(gp, ccb, io["cc"])
        dma(gp, ssb, io["ss"])
        dma(gp, permb, io["perm"])
        wslot_g = nc.sbuf_tensor("wslot", [128, 2 * 2048], BF16)
        qtmp_g = nc.sbuf_tensor("qtmp", [128, 2 * 512], BF16)
        qsw_g = nc.sbuf_tensor("qsw", [128, 2 * 512], BF16)
        ropet_g = nc.sbuf_tensor("ropet", [128, 2 * 512], BF16)
        pgs = [nc.psum_tensor(f"qkv_ps{i}", [128, 512], F32) for i in range(4)]
        swgs = [nc.psum_tensor(f"sw_ps{i}", [128, 512], F32) for i in range(2)]
        wslot = wslot_g.__enter__().ap()
        qtmp = qtmp_g.__enter__().ap()
        qswb = qsw_g.__enter__().ap()
        ropet = ropet_g.__enter__().ap()
        ps = [g.__enter__().ap() for g in pgs]
        sws = [g.__enter__().ap() for g in swgs]

        def rope_evac(psum, tc, dstT, idx, tag):
            pb = idx % 2
            qa = qtmp[:, pb * 512:(pb + 1) * 512]
            act(qa, psum, AF.Copy, tag=tag)
            swp = sws[pb]
            mm(swp, permb, qa, start=True, stop=True, tag=tag + ".sw")
            qs = qswb[:, pb * 512:(pb + 1) * 512]
            act(qs, swp, AF.Copy, tag=tag)
            cs = ccb[:, tc * 512:(tc + 1) * 512]
            sn = ssb[:, tc * 512:(tc + 1) * 512]
            t1 = ropet[:, pb * 512:(pb + 1) * 512]
            tt(t1, qa, cs, ALU.mult, tag)
            tt(qs, qs, sn, ALU.mult, tag)
            tt(dstT[:, tc * 512:(tc + 1) * 512], t1, qs, ALU.add, tag)

        idx = 0
        dma(gp, wslot[:, 0:2048], io["wq"][0], tag="wq0")
        dma(gp, wslot[:, 2048:4096], io["wq"][1], tag="wq1")
        for oc in range(16):
            wsl = wslot[:, (oc % 2) * 2048:(oc % 2) * 2048 + 2048]
            if oc >= 2:
                dma(gp, wsl, io["wq"][oc], tag=f"wq{oc}")
            for tc in range(4):
                pp = ps[idx % 4]
                for i in range(DT):
                    mm(pp, wsl[:, i * 128:(i + 1) * 128],
                       ln1T[:, i * T + tc * 512:i * T + (tc + 1) * 512],
                       start=(i == 0), stop=False, tag=f"q{oc}.{tc}")
                mm(pp, bq[:, oc * 128:(oc + 1) * 128], ones[0:1, 0:512],
                   start=False, stop=True, tag=f"qb{oc}.{tc}")
                rope_evac(pp, tc, qT[:, oc * T:(oc + 1) * T], idx,
                          f"qr{oc}.{tc}")
                idx += 1
        for oc in range(4):
            wsl = wslot[:, (oc % 2) * 2048:(oc % 2) * 2048 + 2048]
            dma(gp, wsl, io["wk"][oc], tag=f"wk{oc}")
            for tc in range(4):
                pp = ps[idx % 4]
                for i in range(DT):
                    mm(pp, wsl[:, i * 128:(i + 1) * 128],
                       ln1T[:, i * T + tc * 512:i * T + (tc + 1) * 512],
                       start=(i == 0), stop=False, tag=f"k{oc}.{tc}")
                mm(pp, bk[:, oc * 128:(oc + 1) * 128], ones[0:1, 0:512],
                   start=False, stop=True, tag=f"kb{oc}.{tc}")
                rope_evac(pp, tc, kT[:, oc * T:(oc + 1) * T], idx,
                          f"kr{oc}.{tc}")
                idx += 1
        for g_ in (ropet_g, qsw_g, qtmp_g, wslot_g, self.perm_g, self.sin_g,
                   self.cos_g, self.bk_g, self.bq_g,
                   *reversed(swgs), *reversed(pgs)):
            g_.__exit__(None, None, None)
        self.bv_g.__exit__(None, None, None)
        ln1T_g.__exit__(None, None, None)

        if "qT" in dbg_t:
            for h in range(NQ):
                dma(sy, dbg_t["qT"][h * 128:(h + 1) * 128, :],
                    qT[:, h * T:(h + 1) * T])
        if "kT" in dbg_t:
            for h in range(NKV):
                dma(sy, dbg_t["kT"][h * 128:(h + 1) * 128, :],
                    kT[:, h * T:(h + 1) * T])
        if "v" in dbg_t:
            for ti in range(TTN):
                dma(sy, dbg_t["v"][ti * 128:(ti + 1) * 128, :],
                    vsb[:, ti * 512:(ti + 1) * 512])

    def p2(self):
        nc, io, dbg_t = self.nc, self.io, self.dbg_t
        gp, sy = "gpsimd", "sync"
        mm, act, tt = self.mm, self.act, self.tt
        dma = self.dma
        ones, qT, kT, vsb = self.ones, self.qT, self.kT, self.vsb

        self.wo_g = nc.sbuf_tensor("wo_sb", [128, 4 * 8192], BF16,
                                   side="right")
        self.wo_sb = self.wo_g.__enter__().ap()
        for oc in range(4):
            dma(gp, self.wo_sb[:, oc * 8192:(oc + 1) * 8192],
                io["wo"][oc].rearrange("p i o -> p (i o)"), tag=f"wo{oc}")
        self.bo_g = nc.sbuf_tensor("s_bo", [1, D], BF16, side="right")
        self.bo = self.bo_g.__enter__().ap()
        dma(gp, self.bo, io["bo"])

        masks = self.masks
        PT_g = nc.sbuf_tensor("PT", [128, 2 * 8192], BF16)
        rl_g = nc.sbuf_tensor("rl", [1, 2 * 512], BF16)
        rlbc_g = nc.sbuf_tensor("rlbc", [128, 2 * 512], BF16)
        sgs = [nc.psum_tensor(f"s_ps{i}", [128, 1024], F32) for i in range(2)]
        ags = [nc.psum_tensor(f"av_ps{i}", [128, 512], F32) for i in range(2)]
        lgs = [nc.psum_tensor(f"l_ps{i}", [128, 512], F32) for i in range(2)]
        PT = PT_g.__enter__().ap()
        rl = rl_g.__enter__().ap()
        rlbc = rlbc_g.__enter__().ap()
        sps = [g.__enter__().ap() for g in sgs]
        avs = [g.__enter__().ap() for g in ags]
        lpss = [g.__enter__().ap() for g in lgs]

        def stexp_pair(h, qb, kn, par, ktp):
            qslice = qT[:, h * T + qb * 512:h * T + (qb + 1) * 512]
            kvh = h // 4
            sp = sps[ktp % 2]
            ptp = PT[:, par * 8192 + ktp * 1024:par * 8192 + (ktp + 1) * 1024]
            for sub in range(2):
                kt = 2 * ktp + sub
                sph = sp[:, sub * 512:(sub + 1) * 512]
                mm(sph, kT[:, kvh * T + kt * 128:kvh * T + (kt + 1) * 128],
                   qslice, start=True, stop=True, tag=f"st{h}.{qb}.{kt}")
                if kt >= qb * 4:
                    mrel = kt - qb * 4
                    tt(sph, sph, masks[:, mrel * 512:(mrel + 1) * 512],
                       ALU.add, tag=f"mask{h}.{qb}.{kt}")
            act(ptp, sp, AF.Exp, scale=SCALE, tag=f"exp{h}.{qb}.{ktp}")

        def lav_kt(h, qb, kn, par, it, kt):
            kvh = h // 4
            av = avs[it % 2]
            lfull = lpss[it % 2]
            pt = PT[:, par * 8192 + kt * 512:par * 8192 + (kt + 1) * 512]
            mm(lfull[0:1, :], ones[:, 0:1], pt, start=(kt == 0),
               stop=(kt == kn - 1), tag=f"l{h}.{qb}.{kt}")
            mm(av, vsb[:, kt * 512 + kvh * 128:kt * 512 + (kvh + 1) * 128],
               pt, start=(kt == 0), stop=(kt == kn - 1),
               tag=f"av{h}.{qb}.{kt}")

        def lav_tail(h, qb, par, it):
            av = avs[it % 2]
            lfull = lpss[it % 2]
            rla = rl[0:1, (it % 2) * 512:(it % 2) * 512 + 512]
            self.vrecip(rla, lfull[0:1, :], tag=f"rl{h}.{qb}")
            mm(lfull, ones[0:1, 0:128], rla, start=True, stop=True,
               tag=f"rlbc{h}.{qb}")
            rb = rlbc[:, (it % 2) * 512:(it % 2) * 512 + 512]
            act(rb, lfull, AF.Copy, tag=f"rlbce{h}.{qb}")
            tt(qT[:, h * T + qb * 512:h * T + (qb + 1) * 512], av, rb,
               ALU.mult, tag=f"ynorm{h}.{qb}")

        units = [(qb, kvh * 4 + g, (qb + 1) * 4)
                 for qb in range(4) for kvh in range(NKV) for g in range(4)]
        pending_tail = None
        for u in range(len(units) + 1):
            cur = units[u] if u < len(units) else None
            prv = units[u - 1] if u >= 1 else None
            npair = cur[2] // 2 if cur else 0
            nkt = prv[2] if prv else 0
            for i in range(max(npair, (nkt + 1) // 2, 1)):
                if cur and i < npair:
                    stexp_pair(cur[1], cur[0], cur[2], u % 2, i)
                if i == 0 and pending_tail is not None:
                    lav_tail(*pending_tail)
                    pending_tail = None
                if prv:
                    for kt in (2 * i, 2 * i + 1):
                        if kt < nkt:
                            lav_kt(prv[1], prv[0], prv[2], (u - 1) % 2,
                                   u - 1, kt)
            if prv:
                pending_tail = (prv[1], prv[0], (u - 1) % 2, u - 1)
        if pending_tail is not None:
            lav_tail(*pending_tail)
        for g_ in (rlbc_g, rl_g, PT_g, *reversed(lgs),
                   *reversed(ags), *reversed(sgs)):
            g_.__exit__(None, None, None)
        self.vsb_g.__exit__(None, None, None)
        self.kT_g.__exit__(None, None, None)
        if "yT" in dbg_t:
            for h in range(NQ):
                dma(sy, dbg_t["yT"][h * 128:(h + 1) * 128, :],
                    qT[:, h * T:(h + 1) * T])

    def p3(self):
        nc, io, dbg_t = self.nc, self.io, self.dbg_t
        sy = "sync"
        mm, act, tt, dma = self.mm, self.act, self.tt, self.dma
        ones, yT, wo_sb, bo = self.ones, self.qT, self.wo_sb, self.bo

        self.x1d = nc.dram_tensor("x1_scratch", [T, D], F32,
                                  kind="Internal").ap()
        self.ln2d = nc.dram_tensor("ln2_scratch", [T, D], BF16,
                                   kind="Internal").ap()
        x1d = self.x1d
        ln2d = self.ln2d
        x3_g = nc.sbuf_tensor("x_t3", [128, 2 * D], F32, side="right")
        ao_g = nc.sbuf_tensor("ao", [128, 2 * D], F32, side="right")
        l23_g = nc.sbuf_tensor("l2p3", [128, 2 * D], BF16, side="right")
        bn3_g = nc.sbuf_tensor("bn3", [128, 2 * 32], F32, side="right")
        st3_g = nc.sbuf_tensor("st3", [128, 2 * 16], F32, side="right")
        wgs = [nc.psum_tensor(f"wo_ps{i}", [128, 512], F32) for i in range(4)]
        x3 = x3_g.__enter__().ap()
        ao = ao_g.__enter__().ap()
        l23 = l23_g.__enter__().ap()
        bn3 = bn3_g.__enter__().ap()
        st3 = st3_g.__enter__().ap()
        wops = [g.__enter__().ap() for g in wgs]
        for ti in range(TTN):
            pb = ti % 2
            aoa = ao[:, pb * D:(pb + 1) * D]
            xa = x3[:, pb * D:(pb + 1) * D]
            dma(sy, xa, io["x"][ti * 128:(ti + 1) * 128, :], tag=f"x3.{ti}")
            for oc in range(4):
                pp = wops[oc]
                for i in range(DT):
                    mm(pp, yT[:, i * T + ti * 128:i * T + (ti + 1) * 128],
                       wo_sb[:, oc * 8192 + i * 512:oc * 8192 + (i + 1) * 512],
                       start=(i == 0), stop=False, tag=f"wo{oc}.{ti}")
                mm(pp, ones[0:1, 0:128], bo[0:1, oc * 512:(oc + 1) * 512],
                   start=False, stop=True, tag=f"wob{oc}.{ti}")
                act(aoa[:, oc * 512:(oc + 1) * 512], pp, AF.Copy,
                    tag=f"woe{oc}.{ti}")
            bnb = bn3[:, pb * 32:pb * 32 + 32]
            agg = st3[:, pb * 16:pb * 16 + 2]
            veps = st3[:, pb * 16 + 2:pb * 16 + 3]
            rstd = st3[:, pb * 16 + 3:pb * 16 + 4]
            negmr = st3[:, pb * 16 + 4:pb * 16 + 5]
            self.ln_stats(aoa, bnb, agg, veps, rstd, negmr, tag=f"ln3.{ti}")
            act(aoa, aoa, AF.Identity, bias=negmr, scale=rstd, tag=f"ln3a.{ti}")
            tt(aoa, aoa, xa, ALU.add, tag=f"res3.{ti}")
            dma(sy, x1d[ti * 128:(ti + 1) * 128, :], aoa, tag=f"x1.{ti}")
            agg2 = st3[:, pb * 16 + 8:pb * 16 + 10]
            veps2 = st3[:, pb * 16 + 10:pb * 16 + 11]
            rstd2 = st3[:, pb * 16 + 11:pb * 16 + 12]
            negmr2 = st3[:, pb * 16 + 12:pb * 16 + 13]
            self.ln_stats(aoa, bnb, agg2, veps2, rstd2, negmr2,
                          tag=f"ln32.{ti}")
            l2a = l23[:, pb * D:(pb + 1) * D]
            act(l2a, aoa, AF.Identity, bias=negmr2, scale=rstd2,
                tag=f"ln32a.{ti}")
            dma(sy, ln2d[ti * 128:(ti + 1) * 128, :], l2a, tag=f"l2s.{ti}")
        for g_ in (st3_g, bn3_g, l23_g, ao_g, x3_g, *reversed(wgs)):
            g_.__exit__(None, None, None)
        self.qT_g.__exit__(None, None, None)
        self.bo_g.__exit__(None, None, None)
        self.wo_g.__exit__(None, None, None)
        if "x1" in dbg_t:
            with nc.sbuf_tensor("x1dbg", [128, D], F32) as xdg:
                xg = xdg.ap()
                for ti in range(TTN):
                    dma(sy, xg, x1d[ti * 128:(ti + 1) * 128, :])
                    dma(sy, dbg_t["x1"][ti * 128:(ti + 1) * 128, :], xg)

    def p4(self):
        nc, io, dbg_t = self.nc, self.io, self.dbg_t
        gp, sy = "gpsimd", "sync"
        mm, act, tt, ts, stt = self.mm, self.act, self.tt, self.ts, self.stt
        dma, transpose, vcopy = self.dma, self.transpose, self.vcopy
        ones = self.ones
        x1d = self.x1d

        wa_g = nc.sbuf_tensor("wa_sb", [128, 16 * 16], BF16)
        rb2_g = nc.sbuf_tensor("rb2_sb", [16, D], BF16)
        sb2a_g = nc.sbuf_tensor("sb2a", [1, D], BF16)
        sb2b_g = nc.sbuf_tensor("sb2b", [1, D], BF16)
        wa_sb = wa_g.__enter__().ap()
        rb2_sb = rb2_g.__enter__().ap()
        sb2a = sb2a_g.__enter__().ap()
        sb2b = sb2b_g.__enter__().ap()
        dma(gp, wa_sb, io["wa"].rearrange("p i e -> p (i e)"))
        dma(gp, rb2_sb, io["rb2"])
        dma(gp, sb2a, io["sb2"][0:1, :])
        dma(gp, sb2b, io["sb2"][1:2, :])

        acc_g = nc.sbuf_tensor("acc", [128, TTH * D], F32)
        ln2T_g = nc.sbuf_tensor("ln2T", [128, DT * TH], BF16)
        acc = acc_g.__enter__().ap()
        ln2T = ln2T_g.__enter__().ap()
        ln2T3 = ln2T.rearrange("p (j t) -> p j t", j=DT)

        names = ["x1_t", "ubuf", "bn4", "st4", "c_sb", "cbf", "cT",
                 "rsc", "wsl4", "ysc", "bgb", "bb1"]
        shapes = [([128, 2 * D], F32), ([128, 2 * D], F32),
                  ([128, 4 * 32], F32), ([128, 4 * 8], F32),
                  ([128, TTH * 16], F32), ([128, 16], BF16),
                  ([16, TH], BF16), ([128, 2 * 64], F32),
                  ([128, 2 * 8192], BF16), ([128, 8 * F], BF16),
                  ([128, 2 * 8], F32), ([128, 2 * 8], F32)]
        guards = [nc.sbuf_tensor(nm, sh, dt)
                  for nm, (sh, dt) in zip(names, shapes)]
        aps = [g.__enter__().ap() for g in guards]
        (x1t, ubuf, bn4, st4, c_sb, cbf, cTs, rsc, wsl4, ysc, bgb,
         bb1) = aps
        x1p = None
        l2t = ysc[:, 0:2 * D]
        fgs = [nc.psum_tensor(f"f_ps{i}", [128, 512], F32) for i in range(6)]
        tgs = [nc.psum_tensor(f"tr4_ps{i}", [128, 1024], BF16)
               for i in range(2)]
        fps = [g.__enter__().ap() for g in fgs]
        trp4 = [g.__enter__().ap() for g in tgs]

        ctx = (x1p, l2t, bn4, st4, c_sb, cbf, cTs, rsc, fps, trp4, acc,
               ln2T, ln2T3, wa_sb)
        self.p4_prep1(0, ctx, interleave_epi=None)
        self.p4_prep23(0, ctx)
        if "ln2T" in dbg_t:
            for i in range(DT):
                dma(sy, dbg_t["ln2T"][i * 128:(i + 1) * 128, :],
                    ln2T[:, i * TH:(i + 1) * TH])
        self.p4_experts(0, wsl4, ysc, bgb, bb1, fps, acc, ln2T, c_sb,
                        cTs, rb2_sb, sb2a, sb2b)
        if "acc" in dbg_t:
            for tl in range(TTH):
                dma(sy, dbg_t["acc"][tl * 128:(tl + 1) * 128, :],
                    acc[:, tl * D:(tl + 1) * D])
        self.p4_prep1(1, ctx, interleave_epi=(x1t, ubuf, bn4, st4, acc))
        self.p4_prep23(1, ctx)
        self.p4_experts(1, wsl4, ysc, bgb, bb1, fps, acc, ln2T, c_sb,
                        cTs, rb2_sb, sb2a, sb2b)
        for tl in range(TTH):
            self.p4_epi_tile(1, tl, x1t, ubuf, bn4, st4, acc, fps)

        for g_ in (list(reversed(guards)) + [ln2T_g, acc_g, sb2b_g, sb2a_g,
                                             rb2_g, wa_g] +
                   list(reversed(tgs)) + list(reversed(fgs))):
            g_.__exit__(None, None, None)

    def p4_prep1(self, hh, ctx, interleave_epi=None):
        (x1p, l2t, bn4, st4, c_sb, cbf, cTs, rsc, fps, trp4, acc,
         ln2T, ln2T3, wa_sb) = ctx
        nc, io, dbg_t = self.nc, self.io, self.dbg_t
        sy = "sync"
        dma, transpose, vcopy = self.dma, self.transpose, self.vcopy
        ln2d = self.ln2d
        for tl in range(TTH):
            if interleave_epi is not None:
                ex1t, eubuf, ebn4, est4, eacc = interleave_epi
                self.p4_epi_tile(hh - 1, tl, ex1t, eubuf, ebn4, est4, eacc,
                                 fps)
            ti = hh * TTH + tl
            la = l2t[:, (tl % 2) * D:(tl % 2 + 1) * D]
            dma("scalar", la, ln2d[ti * 128:(ti + 1) * 128, :], tag=f"l2r{ti}")
            aslice = acc[:, tl * D:(tl + 1) * D]
            vcopy(aslice, la, tag=f"acci.{ti}")
            for half in range(2):
                pt = trp4[half]
                for jj in range(8):
                    j = half * 8 + jj
                    transpose(pt[:, jj * 128:(jj + 1) * 128],
                              la[:, j * 128:(j + 1) * 128], tag=f"tr4.{ti}.{j}")
                dst = ln2T3[:, half * 8:(half + 1) * 8, tl * 128:(tl + 1) * 128]
                src = pt.rearrange("p (j c) -> p j c", j=8)
                vcopy(dst, src, tag=f"trc4.{ti}.{half}")

    def p4_prep23(self, hh, ctx):
        (x1p, l2t, bn4, st4, c_sb, cbf, cTs, rsc, fps, trp4, acc,
         ln2T, ln2T3, wa_sb) = ctx
        nc, io, dbg_t = self.nc, self.io, self.dbg_t
        sy = "sync"
        mm, act, tt, ts, stt = self.mm, self.act, self.tt, self.ts, self.stt
        dma, transpose, vcopy = self.dma, self.transpose, self.vcopy
        for tl in range(TTH):
            ti = hh * TTH + tl
            pa = fps[tl % 2]
            for i in range(DT):
                mm(pa[:, 0:16],
                   ln2T[:, i * TH + tl * 128:i * TH + (tl + 1) * 128],
                   wa_sb[:, i * 16:(i + 1) * 16], start=(i == 0),
                   stop=(i == DT - 1), tag=f"aff{ti}")
            afa = c_sb[:, tl * 16:(tl + 1) * 16]
            act(afa, pa[:, 0:16], AF.Exp, scale=-1.0, tag=f"affs{ti}")
            ts(afa, afa, 1.0, None, ALU.add, tag=f"affs{ti}")
            self.vrecip(afa, afa, tag=f"affs{ti}")
            if "aff" in dbg_t:
                dma(sy, dbg_t["aff"][ti * 128:(ti + 1) * 128, :], afa)
        for tl in range(TTH):
            ti = hh * TTH + tl
            pb = tl % 2
            afa = c_sb[:, tl * 16:(tl + 1) * 16]
            m1 = st4[:, pb * 8 + 5:pb * 8 + 6]
            m2 = st4[:, pb * 8 + 6:pb * 8 + 7]
            rden = st4[:, pb * 8 + 7:pb * 8 + 8]
            msk1 = rsc[:, pb * 64:pb * 64 + 16]
            aff2 = rsc[:, pb * 64 + 16:pb * 64 + 32]
            msk2 = rsc[:, pb * 64 + 32:pb * 64 + 48]
            msum = rsc[:, pb * 64 + 48:pb * 64 + 64]
            self.vop(lambda e, o=m1, i=afa: e.reduce_max(o, i, AXX),
                     [afa], [m1], tag=f"rt{ti}")
            ts(msk1, afa, m1, None, ALU.is_equal, tag=f"rt{ti}")
            stt(aff2, msk1, NEG, afa, ALU.mult, ALU.add, tag=f"rt{ti}")
            self.vop(lambda e, o=m2, i=aff2: e.reduce_max(o, i, AXX),
                     [aff2], [m2], tag=f"rt{ti}")
            ts(msk2, aff2, m2, None, ALU.is_equal, tag=f"rt{ti}")
            tt(msum, msk1, msk2, ALU.add, tag=f"rt{ti}")
            tt(m1, m1, m2, ALU.add, tag=f"rt{ti}")
            self.vrecip(rden, m1, tag=f"rt{ti}")
            stt(afa, afa, rden, msum, ALU.mult, ALU.mult, tag=f"rt{ti}")
            if "c" in dbg_t:
                dma(sy, dbg_t["c"][ti * 128:(ti + 1) * 128, :], afa)
            vcopy(cbf, afa, tag=f"cb{ti}")
            ctp = trp4[tl % 2]
            transpose(ctp[0:16, 0:128], cbf, tag=f"ctr{ti}")
            vcopy(cTs[:, tl * 128:(tl + 1) * 128], ctp[0:16, 0:128],
                  tag=f"ctc{ti}")

    def p4_experts(self, hh, wsl4, ysc, bgb, bb1, fps, acc, ln2T, c_sb,
                   cTs, rb2_sb, sb2a, sb2b):
        io = self.io
        gp, sy = "gpsimd", "sync"
        mm, act, tt, stt = self.mm, self.act, self.tt, self.stt
        dma = self.dma
        ones = self.ones
        par = 0
        for e in range(NE + NSH):
            routed = e < NE
            ei = e if routed else e - NE
            wsrc = (io["rg"], io["rw1"], io["rw2"]) if routed else \
                   (io["sg"], io["sw1"], io["sw2"])
            bsrc = (io["rgb"], io["rb1"]) if routed else (io["sgb"], io["sb1"])
            ebp = e % 2
            dma(sy, bgb[:, ebp * 8:ebp * 8 + 8], bsrc[0][ei], tag=f"bgb{e}")
            dma(sy, bb1[:, ebp * 8:ebp * 8 + 8], bsrc[1][ei], tag=f"bb1{e}")
            for fh in range(2):
                w = wsl4[:, (par % 2) * 8192:(par % 2) * 8192 + 8192]
                dma(gp, w, wsrc[0][ei, fh].rearrange("p i f -> p (i f)"),
                    tag=f"g{e}.{fh}")
                par += 1
                for fc in range(4):
                    fg = fh * 4 + fc
                    for tc in range(2):
                        pp = fps[2 + (fg * 2 + tc) % 2]
                        for i in range(DT):
                            mm(pp,
                               w[:, i * 512 + fc * 128:i * 512 + (fc + 1) * 128],
                               ln2T[:, i * TH + tc * 512:i * TH + (tc + 1) * 512],
                               start=(i == 0), stop=(i == DT - 1),
                               tag=f"gm{e}.{fg}.{tc}")
                        gb_ap = bgb[:, ebp * 8 + fg:ebp * 8 + fg + 1]
                        act(ysc[:, fg * F + tc * 512:fg * F + (tc + 1) * 512],
                            pp, AF.Gelu, bias=gb_ap, tag=f"ge{e}.{fg}.{tc}")
            for fh in range(2):
                w = wsl4[:, (par % 2) * 8192:(par % 2) * 8192 + 8192]
                dma(gp, w, wsrc[1][ei, fh].rearrange("p i f -> p (i f)"),
                    tag=f"w1{e}.{fh}")
                par += 1
                for fc in range(4):
                    fg = fh * 4 + fc
                    for tc in range(2):
                        pp = fps[2 + (fg * 2 + tc) % 2]
                        for i in range(DT):
                            mm(pp,
                               w[:, i * 512 + fc * 128:i * 512 + (fc + 1) * 128],
                               ln2T[:, i * TH + tc * 512:i * TH + (tc + 1) * 512],
                               start=(i == 0), stop=(i == DT - 1),
                               tag=f"w1m{e}.{fg}.{tc}")
                        b1_ap = bb1[:, ebp * 8 + fg:ebp * 8 + fg + 1]
                        ya = ysc[:, fg * F + tc * 512:fg * F + (tc + 1) * 512]
                        stt(ya, pp, b1_ap, ya, ALU.add, ALU.mult,
                            tag=f"w1e{e}.{fg}.{tc}")
            for fh in range(2):
                w = wsl4[:, (par % 2) * 8192:(par % 2) * 8192 + 8192]
                dma(gp, w, wsrc[2][ei, fh].rearrange("p j d -> p (j d)"),
                    tag=f"w2{e}.{fh}")
                par += 1
                special = (e == NE and fh == 0)
                special2 = (e == NE + 1 and fh == 0)
                for tl in range(TTH):
                    for dc in range(4):
                        pp = fps[2 + (tl * 4 + dc) % 4]
                        for jj in range(4):
                            fg = fh * 4 + jj
                            mm(pp,
                               ysc[:, fg * F + tl * 128:fg * F + (tl + 1) * 128],
                               w[:, jj * D + dc * 512:jj * D + (dc + 1) * 512],
                               start=(jj == 0),
                               stop=(jj == 3 and not (special or special2)),
                               tag=f"w2m{e}.{fh}.{tl}.{dc}")
                        if special:
                            mm(pp, ones[0:1, 0:128],
                               sb2a[0:1, dc * 512:(dc + 1) * 512],
                               start=False, stop=False, tag=f"sb2a.{tl}.{dc}")
                            mm(pp, cTs[:, tl * 128:(tl + 1) * 128],
                               rb2_sb[:, dc * 512:(dc + 1) * 512],
                               start=False, stop=True, tag=f"rb2.{tl}.{dc}")
                        if special2:
                            mm(pp, ones[0:1, 0:128],
                               sb2b[0:1, dc * 512:(dc + 1) * 512],
                               start=False, stop=True, tag=f"sb2b.{tl}.{dc}")
                        aslice = acc[:, tl * D + dc * 512:tl * D + (dc + 1) * 512]
                        if routed:
                            ce = c_sb[:, tl * 16 + ei:tl * 16 + ei + 1]
                            stt(aslice, pp, ce, aslice, ALU.mult, ALU.add,
                                tag=f"w2e{e}.{fh}.{tl}.{dc}")
                        else:
                            tt(aslice, aslice, pp, ALU.add,
                               tag=f"w2e{e}.{fh}.{tl}.{dc}")


    def p4_epi_tile(self, hh, tl, x1t, ubuf, bn4, st4, acc, fps):
        sy = "sync"
        act, dma = self.act, self.dma
        x1d, out = self.x1d, self.out
        ti = hh * TTH + tl
        pb = tl % 2
        xa = x1t[:, pb * D:(pb + 1) * D]
        ep = ubuf[:, pb * D:(pb + 1) * D]
        dma(sy, xa, x1d[ti * 128:(ti + 1) * 128, :], tag=f"x1e{ti}")
        aslice = acc[:, tl * D:(tl + 1) * D]
        parts = bn4[:, pb * 32:pb * 32 + 10]
        veps = st4[:, pb * 8 + 2:pb * 8 + 3]
        rstd = st4[:, pb * 8 + 3:pb * 8 + 4]
        negmr = st4[:, pb * 8 + 4:pb * 8 + 5]
        self.ln_stats_scalar(aslice, (fps[4], fps[5]), parts, veps, rstd,
                             negmr, tag=f"lnE.{ti}")
        act(ep, aslice, AF.Identity, bias=negmr, scale=rstd, tag=f"lnEa.{ti}")
        self.S.op("gpsimd",
                  lambda e, o=ep, a=ep, b=xa: e.tensor_tensor(o, a, b, ALU.add),
                  reads=[ep, xa], writes=[ep], tag=f"resE.{ti}")
        dma(sy, out[ti * 128:(ti + 1) * 128, :], ep, tag=f"out{ti}")


def build_nc(dbg=(), stop_after=None):
    nc = bass.Bass("TRN2", target_bir_lowering=False, debug=False)
    io = {}
    for name, shape, dt in INPUT_SPECS:
        io[name] = nc.dram_tensor(name, shape, dt, kind="ExternalInput").ap()
    out = nc.dram_tensor("out", [T, D], F32, kind="ExternalOutput").ap()
    dbg_t = {}
    for name in dbg:
        shp, dt = DBG_SPECS[name]
        dbg_t[name] = nc.dram_tensor("dbg_" + name, shp, dt,
                                     kind="ExternalOutput").ap()
    b = Builder(nc, io, out, dbg_t)
    S = b.build(stop_after=stop_after)
    return nc, S


from concourse.bass_utils import run_bass_kernel_spmd


def kernel(**inputs):
    shared = prep_shared(inputs)
    nc, _ = build_nc()
    in_maps = []
    for b in range(8):
        m = dict(shared)
        m["x"] = np.ascontiguousarray(inputs["L0_attn_x"][b])
        in_maps.append(m)
    res = run_bass_kernel_spmd(nc, in_maps, core_ids=list(range(8)))
    return np.stack([r["out"] for r in res.results], axis=0)


# version 12
# speedup vs baseline: 1.0125x; 1.0125x over previous
import numpy as np
import ml_dtypes
import concourse.bass as bass
import concourse.mybir as mybir

F32 = mybir.dt.float32
BF16 = mybir.dt.bfloat16
AF = mybir.ActivationFunctionType
ALU = mybir.AluOpType
AXX = mybir.AxisListType.X

T = 2048
D = 2048
NQ, NKV, HD, KV = 16, 4, 128, 512
F = 1024
NE, NSH = 16, 2
DT = D // 128
TTN = T // 128
TH = T // 2
TTH = TH // 128
SCALE = float(1.0 / np.sqrt(128.0))
EPS = 1e-5
NEG = -1e30

ENGINES = ("tensor", "vector", "scalar", "gpsimd", "sync")
QUEUES = {"sync": "q_sync", "gpsimd": "q_gpsimd", "scalar": "q_scalar"}
NPAR = 4
QKEYS = tuple(f"{q}_{p}" for q in QUEUES.values() for p in range(NPAR))
ALL_KEYS = ENGINES + QKEYS
PAGE = 4096


class _Op:
    __slots__ = ("engine", "fn", "idx", "deps", "inc_needed", "dma", "clock",
                 "tag", "qidx")

    def __init__(self, engine, fn, idx, dma, tag):
        self.engine, self.fn, self.idx, self.dma, self.tag = \
            engine, fn, idx, dma, tag
        self.deps = []
        self.inc_needed = False
        self.clock = None


class Sched:
    def __init__(self, nc, same_engine_sync=True):
        self.nc = nc
        self.same_engine_sync = same_engine_sync
        self.ops = {e: [] for e in ENGINES}
        self.dma_counts = {q: 0 for q in QUEUES.values()}
        self.dma_issue = {q: [] for q in QUEUES.values()}
        self._frontier = {}
        self._pages = {}
        self._addr_cache = {}

    def _tensor_info(self, name):
        info = self._addr_cache.get(name)
        if info is not None:
            return info
        for alloc in self.nc.m.functions[0].allocations:
            mls = getattr(alloc, "memorylocations", None)
            if not mls:
                continue
            for ml in mls:
                if ml.name not in self._addr_cache:
                    self._addr_cache[ml.name] = (str(ml.type), ml.addr, ml.bank)
        return self._addr_cache[name]

    def _extent(self, ap):
        t = ap.tensor
        typ, addr, bank = self._tensor_info(t.name)
        dsz = mybir.dt.size(ap.dtype)
        aps = list(ap.ap)
        off = ap.offset
        lo = hi = 0
        if typ in ("SB", "PSUM"):
            pstride = aps[0][0]
            base_p = ap.base_partition()
            pcnt = aps[0][1]
            for step, cnt in aps[1:]:
                if cnt <= 1:
                    continue
                d = step * (cnt - 1)
                if d >= 0:
                    hi += d
                else:
                    lo += d
            elem0 = off - base_p * pstride
            if typ == "SB":
                return ("SBUF", base_p, base_p + pcnt,
                        addr + (elem0 + lo) * dsz,
                        addr + (elem0 + hi) * dsz + dsz)
            b_lo = bank * 2048 + (elem0 + lo) * dsz
            b_hi = bank * 2048 + (elem0 + hi) * dsz + dsz
            return ("PSUM", 0, 128, (b_lo // 2048) * 2048,
                    ((b_hi + 2047) // 2048) * 2048)
        for step, cnt in aps:
            if cnt <= 1:
                continue
            d = step * (cnt - 1)
            if d >= 0:
                hi += d
            else:
                lo += d
        return ("DRAM:" + t.name, 0, 128, (off + lo) * dsz,
                (off + hi) * dsz + dsz)

    def op(self, engine, fn, reads=(), writes=(), tag=""):
        return self._rec(engine, fn, reads, writes, None, tag)

    def dma(self, engine, fn, reads=(), writes=(), tag=""):
        return self._rec(engine, fn, reads, writes, QUEUES[engine], tag)

    def _rec(self, engine, fn, reads, writes, dma, tag):
        ops = self.ops[engine]
        o = _Op(engine, fn, len(ops), dma, tag)
        if ops:
            prev = ops[-1]
            o.clock = dict(prev.clock)
            if engine == "tensor":
                o.clock[engine] = prev.idx + 1
        else:
            o.clock = {k: 0 for k in ALL_KEYS}
        deps = {}

        def add(owner):
            key, idx = owner
            cnt = idx + 1
            if cnt > deps.get(key, 0):
                deps[key] = cnt

        rex = [self._extent(ap) for ap in reads]
        wex = [self._extent(ap) for ap in writes]
        for ext in rex:
            for ent in self._conflicts(ext):
                if ent[0] is not None:
                    add(ent[0])
        for ext in wex:
            for ent in self._conflicts(ext):
                if ent[0] is not None:
                    add(ent[0])
                for r in ent[1]:
                    add(r)

        if dma is not None:
            j = self.dma_counts[dma]
            if j >= NPAR:
                deps_key = f"{dma}_{j % NPAR}"
                if deps.get(deps_key, 0) < j // NPAR:
                    deps[deps_key] = j // NPAR

        final = []
        for key, cnt in deps.items():
            if o.clock.get(key, 0) >= cnt:
                continue
            final.append((key, cnt))
        for key, cnt in final:
            if key in ENGINES:
                prod = self.ops[key][cnt - 1]
                assert prod.dma is None, f"engine dep onto dma op {prod.tag}"
                self._merge(o.clock, prod.clock)
                if o.clock[key] < cnt:
                    o.clock[key] = cnt
                prod.inc_needed = True
            else:
                qname, p = key.rsplit("_", 1)
                prod = self.dma_issue[qname][(cnt - 1) * NPAR + int(p)]
                self._merge(o.clock, prod.clock)
                if o.clock[key] < cnt:
                    o.clock[key] = cnt
        o.deps = final

        if dma is None:
            owner = (engine, o.idx)
        else:
            j = self.dma_counts[dma]
            owner = (f"{dma}_{j % NPAR}", j // NPAR)
        for ext in rex:
            self._register(ext, owner, False)
        for ext in wex:
            self._register(ext, owner, True)
        if dma is not None:
            self.dma_counts[dma] += 1
            self.dma_issue[dma].append(o)
            o.qidx = self.dma_counts[dma] - 1
        ops.append(o)
        return o

    def _conflicts(self, ext):
        space = ext[0]
        iv = ext[1:]
        fr = self._frontier.get(space)
        if fr is None:
            return ()
        out = []
        pages = self._pages[space]
        seen = set()
        for pg in range(iv[2] // PAGE, (iv[3] - 1) // PAGE + 1):
            for other in pages.get(pg, ()):
                if other in seen:
                    continue
                seen.add(other)
                if self._overlap(iv, other):
                    e = fr.get(other)
                    if e is not None:
                        out.append(e)
        return out

    @staticmethod
    def _overlap(a, b):
        return not (a[1] <= b[0] or b[1] <= a[0] or a[3] <= b[2] or b[3] <= a[2])

    @staticmethod
    def _covers(a, b):
        return a[0] <= b[0] and a[1] >= b[1] and a[2] <= b[2] and a[3] >= b[3]

    def _register(self, ext, owner, is_write):
        space = ext[0]
        iv = ext[1:]
        fr = self._frontier.setdefault(space, {})
        pages = self._pages.setdefault(space, {})
        if is_write:
            drop = set()
            for pg in range(iv[2] // PAGE, (iv[3] - 1) // PAGE + 1):
                for other in pages.get(pg, ()):
                    if other != iv and self._covers(iv, other):
                        drop.add(other)
            if drop:
                for other in drop:
                    fr.pop(other, None)
                for pg in list(pages):
                    pages[pg] = [x for x in pages[pg] if x not in drop]
            fr[iv] = (owner, [])
        else:
            e = fr.get(iv)
            if e is None:
                fr[iv] = (None, [owner])
            else:
                e[1].append(owner)
        for pg in range(iv[2] // PAGE, (iv[3] - 1) // PAGE + 1):
            lst = pages.setdefault(pg, [])
            if iv not in lst:
                lst.append(iv)

    @staticmethod
    def _merge(clock, other):
        for k, v in other.items():
            if clock.get(k, 0) < v:
                clock[k] = v

    def emit(self):
        nc = self.nc
        sems = {}
        for key in ALL_KEYS:
            g = nc.semaphore(f"sem_{key}")
            sems[key] = g.__enter__()
        inc_upto = {}
        for e in ENGINES:
            lst = []
            c = 0
            for o in self.ops[e]:
                if o.inc_needed and o.dma is None:
                    c += 1
                lst.append(c)
            inc_upto[e] = lst

        def body(eng, ename):
            watermark = {k: 0 for k in ALL_KEYS}
            for o in self.ops[ename]:
                for key, cnt in o.deps:
                    if key in ENGINES:
                        if key == ename and key == "tensor":
                            continue
                        val = inc_upto[key][cnt - 1]
                        if val <= watermark[key] or val == 0:
                            continue
                        eng.wait_ge(sems[key], val)
                        watermark[key] = val
                    else:
                        val = 16 * cnt
                        if val <= watermark[key]:
                            continue
                        eng.wait_ge(sems[key], val)
                        watermark[key] = val
                inst = o.fn(eng)
                if o.dma is not None:
                    inst.then_inc(sems[f"{o.dma}_{o.qidx % NPAR}"], 16)
                elif o.inc_needed:
                    inst.then_inc(sems[ename], 1)
            q = QUEUES.get(ename)
            if q is not None and self.dma_counts[q] > 0:
                n = self.dma_counts[q]
                for p in range(NPAR):
                    cnt = (n - p + NPAR - 1) // NPAR
                    if cnt > 0 and watermark[f"{q}_{p}"] < 16 * cnt:
                        eng.wait_ge(sems[f"{q}_{p}"], 16 * cnt)

        with self.nc.Block() as block:
            block.tensor(lambda eng: body(eng, "tensor"))
            block.vector(lambda eng: body(eng, "vector"))
            block.scalar(lambda eng: body(eng, "scalar"))
            block.gpsimd(lambda eng: body(eng, "gpsimd"))
            block.sync(lambda eng: body(eng, "sync"))
        return sems

    def stats(self):
        return {"ops": {e: len(self.ops[e]) for e in ENGINES},
                "deps": sum(len(o.deps) for e in ENGINES for o in self.ops[e]),
                "dmas": dict(self.dma_counts)}


def prep_shared(ins):
    bf = ml_dtypes.bfloat16
    s = {}
    s["wq"] = np.ascontiguousarray(
        ins["L0_attn_Wq"].reshape(16, 128, 16, 128).transpose(0, 3, 2, 1))
    s["wk"] = np.ascontiguousarray(
        ins["L0_attn_Wk"].reshape(4, 128, 16, 128).transpose(0, 3, 2, 1))
    s["wv"] = np.ascontiguousarray(
        ins["L0_attn_Wv"].reshape(512, 16, 128).transpose(2, 1, 0))
    s["wo"] = np.ascontiguousarray(
        ins["L0_attn_Wo"].reshape(4, 512, 16, 128).transpose(0, 3, 2, 1))
    s["bq"] = ins["L0_attn_bq"].reshape(1, D)
    s["bk"] = ins["L0_attn_bk"].reshape(1, KV)
    s["bv"] = ins["L0_attn_bv"].reshape(1, KV)
    s["bo"] = ins["L0_attn_bo"].reshape(1, D)
    s["wa"] = np.ascontiguousarray(
        ins["L0_ffn_wa"].reshape(16, 128, 16).transpose(1, 0, 2))
    def gw(x, ne):
        return np.ascontiguousarray(
            x.reshape(ne, 16, 128, 2, 512).transpose(0, 3, 2, 1, 4))

    def w2w(x, ne):
        return np.ascontiguousarray(
            x.reshape(ne, 2, 4, 128, D).transpose(0, 1, 3, 2, 4))

    s["rg"] = gw(ins["L0_ffn_rg"], NE)
    s["rw1"] = gw(ins["L0_ffn_rw1"], NE)
    s["rw2"] = w2w(ins["L0_ffn_rw2"], NE)
    s["sg"] = gw(ins["L0_ffn_sg"], NSH)
    s["sw1"] = gw(ins["L0_ffn_sw1"], NSH)
    s["sw2"] = w2w(ins["L0_ffn_sw2"], NSH)
    s["rgb"] = np.ascontiguousarray(
        ins["L0_ffn_rgb"].reshape(NE, 8, 128).transpose(0, 2, 1))
    s["rb1"] = np.ascontiguousarray(
        ins["L0_ffn_rb1"].reshape(NE, 8, 128).transpose(0, 2, 1))
    s["rb2"] = np.ascontiguousarray(ins["L0_ffn_rb2"])
    s["sgb"] = np.ascontiguousarray(
        ins["L0_ffn_sgb"].reshape(NSH, 8, 128).transpose(0, 2, 1))
    s["sb1"] = np.ascontiguousarray(
        ins["L0_ffn_sb1"].reshape(NSH, 8, 128).transpose(0, 2, 1))
    s["sb2"] = np.ascontiguousarray(ins["L0_ffn_sb2"])

    j = np.arange(64, dtype=np.float64)
    th = 10000.0 ** (-2.0 * j / 128.0)
    ang = np.arange(1, T + 1, dtype=np.float64)[:, None] * th
    cosT = np.cos(ang).T
    sinT = np.sin(ang).T
    s["cc"] = np.ascontiguousarray(
        np.concatenate([cosT, cosT], axis=0)).astype(bf)
    s["ss"] = np.ascontiguousarray(
        np.concatenate([-sinT, sinT], axis=0)).astype(bf)
    pm = np.zeros((128, 128), dtype=bf)
    for mcol in range(128):
        pm[(mcol + 64) % 128, mcol] = 1.0
    s["perm"] = pm
    s["ident"] = np.eye(128, dtype=bf)
    s["ones"] = np.ones((128, 512), dtype=bf)
    mk = np.zeros((128, 4, 512), dtype=np.float32)
    p = np.arange(128)[:, None]
    jj = np.arange(512)[None, :]
    for mm_ in range(4):
        mk[:, mm_, :] = np.where(mm_ * 128 + p > jj, NEG, 0.0)
    s["masks"] = np.ascontiguousarray(mk.reshape(128, 4 * 512)).astype(bf)
    return s


INPUT_SPECS = [
    ("x", [T, D], F32),
    ("wq", [16, 128, 16, 128], F32),
    ("wk", [4, 128, 16, 128], F32),
    ("wv", [128, 16, 512], F32),
    ("wo", [4, 128, 16, 512], F32),
    ("bq", [1, D], F32),
    ("bk", [1, KV], F32),
    ("bv", [1, KV], F32),
    ("bo", [1, D], F32),
    ("wa", [128, 16, 16], F32),
    ("rg", [NE, 2, 128, 16, 512], F32),
    ("rw1", [NE, 2, 128, 16, 512], F32),
    ("rw2", [NE, 2, 128, 4, D], F32),
    ("sg", [NSH, 2, 128, 16, 512], F32),
    ("sw1", [NSH, 2, 128, 16, 512], F32),
    ("sw2", [NSH, 2, 128, 4, D], F32),
    ("rgb", [NE, 128, 8], F32),
    ("rb1", [NE, 128, 8], F32),
    ("rb2", [NE, D], F32),
    ("sgb", [NSH, 128, 8], F32),
    ("sb1", [NSH, 128, 8], F32),
    ("sb2", [NSH, D], F32),
    ("cc", [128, T], BF16),
    ("ss", [128, T], BF16),
    ("perm", [128, 128], BF16),
    ("ident", [128, 128], BF16),
    ("ones", [128, 512], BF16),
    ("masks", [128, 4 * 512], BF16),
]

DBG_SPECS = {
    "ln1T": ([D, T], BF16), "qT": ([D, T], BF16), "kT": ([KV, T], BF16),
    "v": ([T, KV], BF16), "yT": ([D, T], BF16), "x1": ([T, D], F32),
    "ln2T": ([D, TH], BF16), "aff": ([T, 16], F32), "c": ([T, 16], F32),
    "acc": ([T, D], F32),
}




class Builder:
    def __init__(self, nc, io, out, dbg_t):
        self.nc = nc
        self.io = io
        self.out = out
        self.dbg_t = dbg_t
        self.S = Sched(nc)

    def dma(self, eng, dst, src, tag=""):
        self.S.dma(eng, lambda e, d=dst, s=src: e.dma_start(out=d, in_=s),
                   reads=[src], writes=[dst], tag=tag)

    def mm(self, out_ap, lhsT, rhs, start, stop, tag=""):
        self.S.op("tensor",
                  lambda e, o=out_ap, l=lhsT, r=rhs, st=start, sp=stop:
                  e.matmul(o, l, r, start=st, stop=sp),
                  reads=[lhsT, rhs], writes=[out_ap], tag=tag)

    def transpose(self, out_ap, in_ap, tag=""):
        ident = self.ident
        self.S.op("tensor",
                  lambda e, o=out_ap, i=in_ap, d=ident: e.transpose(o, i, d),
                  reads=[in_ap, ident], writes=[out_ap], tag=tag)

    def act(self, out_ap, in_ap, func, bias=0.0, scale=1.0, tag=""):
        reads = [in_ap]
        if isinstance(bias, bass.AP):
            reads.append(bias)
        if isinstance(scale, bass.AP):
            reads.append(scale)
        self.S.op("scalar",
                  lambda e, o=out_ap, i=in_ap, f=func, b=bias, s=scale:
                  e.activation(o, i, f, bias=b, scale=s),
                  reads=reads, writes=[out_ap], tag=tag)

    def vop(self, fn, reads, writes, tag=""):
        self.S.op("vector", fn, reads=reads, writes=writes, tag=tag)

    def tt(self, out_ap, a, b, opk, tag=""):
        self.vop(lambda e, o=out_ap, x=a, y=b, k=opk:
                 e.tensor_tensor(o, x, y, k), [a, b], [out_ap], tag)

    def ts(self, out_ap, a, s1, s2, op0, op1=ALU.bypass, tag=""):
        reads = [a] + [s for s in (s1, s2) if isinstance(s, bass.AP)]
        self.vop(lambda e, o=out_ap, x=a, u=s1, w=s2, k0=op0, k1=op1:
                 e.tensor_scalar(o, x, u, w, k0, k1), reads, [out_ap], tag)

    def stt(self, out_ap, in0, scalar, in1, op0, op1, tag=""):
        reads = [in0, in1] + ([scalar] if isinstance(scalar, bass.AP) else [])
        self.vop(lambda e, o=out_ap, a=in0, s=scalar, b=in1, k0=op0, k1=op1:
                 e.scalar_tensor_tensor(o, a, s, b, k0, k1),
                 reads, [out_ap], tag)

    def vcopy(self, out_ap, in_ap, tag=""):
        self.vop(lambda e, o=out_ap, i=in_ap: e.tensor_copy(o, i),
                 [in_ap], [out_ap], tag)

    def vrecip(self, out_ap, in_ap, tag=""):
        nc = self.nc

        def fn(e, o=out_ap, i=in_ap):
            if o.dtype == BF16:
                with nc.allow_low_precision("bf16 softmax scale operand"):
                    return e.reciprocal(o, i)
            return e.reciprocal(o, i)

        self.vop(fn, [in_ap], [out_ap], tag)

    def sact(self, out_ap, in_ap, func, scale=1.0, tag=""):
        self.S.op("scalar",
                  lambda e, o=out_ap, i=in_ap, f=func, s=scale:
                  e.activation(o, i, f, scale=s),
                  reads=[in_ap], writes=[out_ap], tag=tag)

    def ln_stats_scalar(self, xa, dump_ps, parts, veps, rstd, negmr,
                        n=D, tag=""):
        nch = n // 512
        for ch in range(nch):
            ia = xa[:, ch * 512:(ch + 1) * 512]
            pa = parts[:, ch:ch + 1]
            dp = dump_ps[ch % 2]
            self.S.op("scalar",
                      lambda e, o=dp, i=ia, a=pa:
                      e.activation(o, i, AF.Square, accum_out=a),
                      reads=[ia], writes=[dp, pa], tag=tag)
        for ch in range(nch):
            ia = xa[:, ch * 512:(ch + 1) * 512]
            pa = parts[:, 4 + ch:5 + ch]
            dp = dump_ps[ch % 2]
            self.S.op("scalar",
                      lambda e, o=dp, i=ia, a=pa:
                      e.activation(o, i, AF.Copy, accum_out=a),
                      reads=[ia], writes=[dp, pa], tag=tag)
        s2 = parts[:, 8:9]
        m = parts[:, 9:10]
        self.vop(lambda e, o=s2, i=parts[:, 0:4]: e.reduce_sum(o, i, AXX),
                 [parts[:, 0:4]], [s2], tag)
        self.vop(lambda e, o=m, i=parts[:, 4:8]: e.reduce_sum(o, i, AXX),
                 [parts[:, 4:8]], [m], tag)
        self.ts(m, m, 1.0 / n, None, ALU.mult, tag=tag)
        self.stt(veps, m, -1.0, m, ALU.mult, ALU.mult, tag=tag)
        self.stt(veps, s2, 1.0 / n, veps, ALU.mult, ALU.add, tag=tag)
        self.ts(veps, veps, EPS, None, ALU.add, tag=tag)
        self.sact(veps, veps, AF.Ln, tag=tag)
        self.sact(rstd, veps, AF.Exp, scale=-0.5, tag=tag)
        self.stt(negmr, m, -1.0, rstd, ALU.mult, ALU.mult, tag=tag)

    def ln_stats(self, xa, bnb, agg, veps, rstd, negmr, n=D, tag=""):
        for ch in range(n // 512):
            ob = bnb[:, ch * 6:(ch + 1) * 6]
            ia = xa[:, ch * 512:(ch + 1) * 512]
            self.vop(lambda e, o=ob, i=ia: e.bn_stats(o, i), [ia], [ob], tag)
        bslice = bnb[:, 0:6 * (n // 512)]
        self.vop(lambda e, o=agg, i=bslice: e.bn_aggr(o, i),
                 [bslice], [agg], tag)
        self.ts(veps, agg[:, 1:2], EPS, None, ALU.add, tag=tag)
        self.sact(veps, veps, AF.Ln, tag=tag)
        self.sact(rstd, veps, AF.Exp, scale=-0.5, tag=tag)
        self.stt(negmr, agg[:, 0:1], -1.0, rstd, ALU.mult, ALU.mult, tag=tag)

    def build(self, stop_after=None):
        nc = self.nc
        io = self.io
        gp, sy = "gpsimd", "sync"

        self.ones_g = nc.sbuf_tensor("s_ones", [128, 512], BF16)
        self.ident_g = nc.sbuf_tensor("s_ident", [128, 128], BF16)
        self.masks_g = nc.sbuf_tensor("s_masks", [128, 4 * 512], BF16,
                                      side="right")
        self.ones = self.ones_g.__enter__().ap()
        self.ident = self.ident_g.__enter__().ap()
        self.masks = self.masks_g.__enter__().ap()
        self.dma(gp, self.ones, io["ones"])
        self.dma(gp, self.ident, io["ident"])
        self.dma(gp, self.masks, io["masks"])

        self.p1()
        if stop_after == "P1":
            self.S.emit()
            return self.S
        self.p2()
        if stop_after == "P2":
            self.S.emit()
            return self.S
        self.p3()
        if stop_after == "P3":
            self.S.emit()
            return self.S
        self.p4()
        self.S.emit()
        return self.S

    def p1(self):
        nc, io, dbg_t = self.nc, self.io, self.dbg_t
        gp, sy = "gpsimd", "sync"
        mm, act, tt, vcopy = self.mm, self.act, self.tt, self.vcopy
        dma, transpose = self.dma, self.transpose
        ones = self.ones

        self.qT_g = nc.sbuf_tensor("qT", [128, NQ * T], BF16)
        self.kT_g = nc.sbuf_tensor("kT", [128, NKV * T], BF16)
        self.vsb_g = nc.sbuf_tensor("vsb", [128, TTN * KV], BF16)
        self.qT = self.qT_g.__enter__().ap()
        self.kT = self.kT_g.__enter__().ap()
        self.vsb = self.vsb_g.__enter__().ap()
        qT, kT, vsb = self.qT, self.kT, self.vsb

        ln1T_g = nc.sbuf_tensor("ln1T", [128, DT * T], BF16)
        ln1T = ln1T_g.__enter__().ap()

        self.bv_g = nc.sbuf_tensor("s_bv", [1, KV], BF16)
        bv = self.bv_g.__enter__().ap()
        dma(gp, bv, io["bv"])
        ln1T3 = ln1T.rearrange("p (j t) -> p j t", j=DT)

        wv_g = nc.sbuf_tensor("wv_sb", [128, 16 * 512], BF16)
        wv_sb = wv_g.__enter__().ap()
        dma(gp, wv_sb, io["wv"].rearrange("p i o -> p (i o)"), tag="wv")

        x_t_g = nc.sbuf_tensor("x_t", [128, 2 * D], F32)
        ln1_t_g = nc.sbuf_tensor("ln1_t", [128, D], BF16)
        bn_g = nc.sbuf_tensor("bn_s", [128, 3 * 32], F32)
        st_g = nc.sbuf_tensor("st_s", [128, 3 * 8], F32)
        tp0 = nc.psum_tensor("tr_ps0", [128, 1024], BF16)
        tp1 = nc.psum_tensor("tr_ps1", [128, 1024], BF16)
        vps = [nc.psum_tensor(f"v_ps{i}", [128, 512], F32) for i in range(2)]
        x_t = x_t_g.__enter__().ap()
        ln1_t = ln1_t_g.__enter__().ap()
        bn = bn_g.__enter__().ap()
        sts = st_g.__enter__().ap()
        trp = [tp0.__enter__().ap(), tp1.__enter__().ap()]
        vpsa = [g.__enter__().ap() for g in vps]
        for ti in range(TTN):
            pb = ti % 2
            xa = x_t[:, pb * D:(pb + 1) * D]
            dma(sy, xa, io["x"][ti * 128:(ti + 1) * 128, :], tag=f"x{ti}")
            bnb = bn[:, (ti % 2) * 32:(ti % 2) * 32 + 32]
            agg = sts[:, pb * 8:pb * 8 + 2]
            veps = sts[:, pb * 8 + 2:pb * 8 + 3]
            rstd = sts[:, pb * 8 + 3:pb * 8 + 4]
            negmr = sts[:, pb * 8 + 4:pb * 8 + 5]
            self.ln_stats(xa, bnb, agg, veps, rstd, negmr, tag=f"ln1.{ti}")
            la = ln1_t[:, 0:D]
            act(la, xa, AF.Identity, bias=negmr, scale=rstd, tag=f"ln1a.{ti}")
            for half in range(2):
                pt = trp[half]
                for jj in range(8):
                    j = half * 8 + jj
                    transpose(pt[:, jj * 128:(jj + 1) * 128],
                              la[:, j * 128:(j + 1) * 128], tag=f"tr1.{ti}.{j}")
                dst = ln1T3[:, half * 8:(half + 1) * 8, ti * 128:(ti + 1) * 128]
                src = pt.rearrange("p (j c) -> p j c", j=8)
                vcopy(dst, src, tag=f"trc.{ti}.{half}")
            pp = vpsa[ti % 2]
            for i in range(DT):
                mm(pp, ln1T[:, i * T + ti * 128:i * T + (ti + 1) * 128],
                   wv_sb[:, i * 512:(i + 1) * 512],
                   start=(i == 0), stop=False, tag=f"v{ti}")
            mm(pp, ones[0:1, 0:128], bv[0:1, 0:512], start=False, stop=True,
               tag=f"vb{ti}")
            act(vsb[:, ti * 512:(ti + 1) * 512], pp, AF.Copy, tag=f"ve{ti}")
        for g_ in (st_g, bn_g, ln1_t_g, x_t_g, *reversed(vps), tp1, tp0):
            g_.__exit__(None, None, None)
        wv_g.__exit__(None, None, None)
        if "ln1T" in dbg_t:
            for i in range(DT):
                dma(sy, dbg_t["ln1T"][i * 128:(i + 1) * 128, :],
                    ln1T[:, i * T:(i + 1) * T])

        self.bq_g = nc.sbuf_tensor("s_bq", [1, D], BF16)
        self.bk_g = nc.sbuf_tensor("s_bk", [1, KV], BF16)
        self.cos_g = nc.sbuf_tensor("ccb", [128, T], BF16)
        self.sin_g = nc.sbuf_tensor("ssb", [128, T], BF16)
        self.perm_g = nc.sbuf_tensor("permb", [128, 128], BF16)
        bq = self.bq_g.__enter__().ap()
        bk = self.bk_g.__enter__().ap()
        ccb = self.cos_g.__enter__().ap()
        ssb = self.sin_g.__enter__().ap()
        permb = self.perm_g.__enter__().ap()
        dma(gp, bq, io["bq"])
        dma(gp, bk, io["bk"])
        dma(gp, ccb, io["cc"])
        dma(gp, ssb, io["ss"])
        dma(gp, permb, io["perm"])
        wslot_g = nc.sbuf_tensor("wslot", [128, 2 * 2048], BF16)
        qtmp_g = nc.sbuf_tensor("qtmp", [128, 2 * 512], BF16)
        qsw_g = nc.sbuf_tensor("qsw", [128, 2 * 512], BF16)
        ropet_g = nc.sbuf_tensor("ropet", [128, 2 * 512], BF16)
        pgs = [nc.psum_tensor(f"qkv_ps{i}", [128, 512], F32) for i in range(4)]
        swgs = [nc.psum_tensor(f"sw_ps{i}", [128, 512], F32) for i in range(2)]
        wslot = wslot_g.__enter__().ap()
        qtmp = qtmp_g.__enter__().ap()
        qswb = qsw_g.__enter__().ap()
        ropet = ropet_g.__enter__().ap()
        ps = [g.__enter__().ap() for g in pgs]
        sws = [g.__enter__().ap() for g in swgs]

        def rope_evac(psum, tc, dstT, idx, tag):
            pb = idx % 2
            qa = qtmp[:, pb * 512:(pb + 1) * 512]
            act(qa, psum, AF.Copy, tag=tag)
            swp = sws[pb]
            mm(swp, permb, qa, start=True, stop=True, tag=tag + ".sw")
            qs = qswb[:, pb * 512:(pb + 1) * 512]
            act(qs, swp, AF.Copy, tag=tag)
            cs = ccb[:, tc * 512:(tc + 1) * 512]
            sn = ssb[:, tc * 512:(tc + 1) * 512]
            t1 = ropet[:, pb * 512:(pb + 1) * 512]
            tt(t1, qa, cs, ALU.mult, tag)
            tt(qs, qs, sn, ALU.mult, tag)
            tt(dstT[:, tc * 512:(tc + 1) * 512], t1, qs, ALU.add, tag)

        idx = 0
        dma(gp, wslot[:, 0:2048], io["wq"][0], tag="wq0")
        dma(gp, wslot[:, 2048:4096], io["wq"][1], tag="wq1")
        for oc in range(16):
            wsl = wslot[:, (oc % 2) * 2048:(oc % 2) * 2048 + 2048]
            if oc >= 2:
                dma(gp, wsl, io["wq"][oc], tag=f"wq{oc}")
            for tc in range(4):
                pp = ps[idx % 4]
                for i in range(DT):
                    mm(pp, wsl[:, i * 128:(i + 1) * 128],
                       ln1T[:, i * T + tc * 512:i * T + (tc + 1) * 512],
                       start=(i == 0), stop=False, tag=f"q{oc}.{tc}")
                mm(pp, bq[:, oc * 128:(oc + 1) * 128], ones[0:1, 0:512],
                   start=False, stop=True, tag=f"qb{oc}.{tc}")
                rope_evac(pp, tc, qT[:, oc * T:(oc + 1) * T], idx,
                          f"qr{oc}.{tc}")
                idx += 1
        for oc in range(4):
            wsl = wslot[:, (oc % 2) * 2048:(oc % 2) * 2048 + 2048]
            dma(gp, wsl, io["wk"][oc], tag=f"wk{oc}")
            for tc in range(4):
                pp = ps[idx % 4]
                for i in range(DT):
                    mm(pp, wsl[:, i * 128:(i + 1) * 128],
                       ln1T[:, i * T + tc * 512:i * T + (tc + 1) * 512],
                       start=(i == 0), stop=False, tag=f"k{oc}.{tc}")
                mm(pp, bk[:, oc * 128:(oc + 1) * 128], ones[0:1, 0:512],
                   start=False, stop=True, tag=f"kb{oc}.{tc}")
                rope_evac(pp, tc, kT[:, oc * T:(oc + 1) * T], idx,
                          f"kr{oc}.{tc}")
                idx += 1
        for g_ in (ropet_g, qsw_g, qtmp_g, wslot_g, self.perm_g, self.sin_g,
                   self.cos_g, self.bk_g, self.bq_g,
                   *reversed(swgs), *reversed(pgs)):
            g_.__exit__(None, None, None)
        self.bv_g.__exit__(None, None, None)
        ln1T_g.__exit__(None, None, None)

        if "qT" in dbg_t:
            for h in range(NQ):
                dma(sy, dbg_t["qT"][h * 128:(h + 1) * 128, :],
                    qT[:, h * T:(h + 1) * T])
        if "kT" in dbg_t:
            for h in range(NKV):
                dma(sy, dbg_t["kT"][h * 128:(h + 1) * 128, :],
                    kT[:, h * T:(h + 1) * T])
        if "v" in dbg_t:
            for ti in range(TTN):
                dma(sy, dbg_t["v"][ti * 128:(ti + 1) * 128, :],
                    vsb[:, ti * 512:(ti + 1) * 512])

    def p2(self):
        nc, io, dbg_t = self.nc, self.io, self.dbg_t
        gp, sy = "gpsimd", "sync"
        mm, act, tt = self.mm, self.act, self.tt
        dma = self.dma
        ones, qT, kT, vsb = self.ones, self.qT, self.kT, self.vsb

        self.wo_g = nc.sbuf_tensor("wo_sb", [128, 4 * 8192], BF16,
                                   side="right")
        self.wo_sb = self.wo_g.__enter__().ap()
        for oc in range(4):
            dma(gp, self.wo_sb[:, oc * 8192:(oc + 1) * 8192],
                io["wo"][oc].rearrange("p i o -> p (i o)"), tag=f"wo{oc}")
        self.bo_g = nc.sbuf_tensor("s_bo", [1, D], BF16, side="right")
        self.bo = self.bo_g.__enter__().ap()
        dma(gp, self.bo, io["bo"])

        masks = self.masks
        PT_g = nc.sbuf_tensor("PT", [128, 2 * 8192], BF16)
        rl_g = nc.sbuf_tensor("rl", [1, 2 * 512], BF16)
        rlbc_g = nc.sbuf_tensor("rlbc", [128, 2 * 512], BF16)
        sgs = [nc.psum_tensor(f"s_ps{i}", [128, 1024], F32) for i in range(2)]
        ags = [nc.psum_tensor(f"av_ps{i}", [128, 512], F32) for i in range(2)]
        lgs = [nc.psum_tensor(f"l_ps{i}", [128, 512], F32) for i in range(2)]
        PT = PT_g.__enter__().ap()
        rl = rl_g.__enter__().ap()
        rlbc = rlbc_g.__enter__().ap()
        sps = [g.__enter__().ap() for g in sgs]
        avs = [g.__enter__().ap() for g in ags]
        lpss = [g.__enter__().ap() for g in lgs]

        def stexp_pair(h, qb, kn, par, ktp):
            qslice = qT[:, h * T + qb * 512:h * T + (qb + 1) * 512]
            kvh = h // 4
            sp = sps[ktp % 2]
            ptp = PT[:, par * 8192 + ktp * 1024:par * 8192 + (ktp + 1) * 1024]
            for sub in range(2):
                kt = 2 * ktp + sub
                sph = sp[:, sub * 512:(sub + 1) * 512]
                mm(sph, kT[:, kvh * T + kt * 128:kvh * T + (kt + 1) * 128],
                   qslice, start=True, stop=True, tag=f"st{h}.{qb}.{kt}")
                if kt >= qb * 4:
                    mrel = kt - qb * 4
                    tt(sph, sph, masks[:, mrel * 512:(mrel + 1) * 512],
                       ALU.add, tag=f"mask{h}.{qb}.{kt}")
            act(ptp, sp, AF.Exp, scale=SCALE, tag=f"exp{h}.{qb}.{ktp}")

        def lav_kt(h, qb, kn, par, it, kt):
            kvh = h // 4
            av = avs[it % 2]
            lfull = lpss[it % 2]
            pt = PT[:, par * 8192 + kt * 512:par * 8192 + (kt + 1) * 512]
            mm(lfull[0:1, :], ones[:, 0:1], pt, start=(kt == 0),
               stop=(kt == kn - 1), tag=f"l{h}.{qb}.{kt}")
            mm(av, vsb[:, kt * 512 + kvh * 128:kt * 512 + (kvh + 1) * 128],
               pt, start=(kt == 0), stop=(kt == kn - 1),
               tag=f"av{h}.{qb}.{kt}")

        def lav_tail(h, qb, par, it):
            av = avs[it % 2]
            lfull = lpss[it % 2]
            rla = rl[0:1, (it % 2) * 512:(it % 2) * 512 + 512]
            self.vrecip(rla, lfull[0:1, :], tag=f"rl{h}.{qb}")
            mm(lfull, ones[0:1, 0:128], rla, start=True, stop=True,
               tag=f"rlbc{h}.{qb}")
            rb = rlbc[:, (it % 2) * 512:(it % 2) * 512 + 512]
            act(rb, lfull, AF.Copy, tag=f"rlbce{h}.{qb}")
            tt(qT[:, h * T + qb * 512:h * T + (qb + 1) * 512], av, rb,
               ALU.mult, tag=f"ynorm{h}.{qb}")

        units = [(qb, kvh * 4 + g, (qb + 1) * 4)
                 for qb in range(4) for kvh in range(NKV) for g in range(4)]
        pending_tail = None
        for u in range(len(units) + 1):
            cur = units[u] if u < len(units) else None
            prv = units[u - 1] if u >= 1 else None
            npair = cur[2] // 2 if cur else 0
            nkt = prv[2] if prv else 0
            for i in range(max(npair, (nkt + 1) // 2, 1)):
                if cur and i < npair:
                    stexp_pair(cur[1], cur[0], cur[2], u % 2, i)
                if i == 0 and pending_tail is not None:
                    lav_tail(*pending_tail)
                    pending_tail = None
                if prv:
                    for kt in (2 * i, 2 * i + 1):
                        if kt < nkt:
                            lav_kt(prv[1], prv[0], prv[2], (u - 1) % 2,
                                   u - 1, kt)
            if prv:
                pending_tail = (prv[1], prv[0], (u - 1) % 2, u - 1)
        if pending_tail is not None:
            lav_tail(*pending_tail)
        for g_ in (rlbc_g, rl_g, PT_g, *reversed(lgs),
                   *reversed(ags), *reversed(sgs)):
            g_.__exit__(None, None, None)
        self.vsb_g.__exit__(None, None, None)
        self.kT_g.__exit__(None, None, None)
        if "yT" in dbg_t:
            for h in range(NQ):
                dma(sy, dbg_t["yT"][h * 128:(h + 1) * 128, :],
                    qT[:, h * T:(h + 1) * T])

    def p3(self):
        nc, io, dbg_t = self.nc, self.io, self.dbg_t
        sy = "sync"
        mm, act, tt, dma = self.mm, self.act, self.tt, self.dma
        ones, yT, wo_sb, bo = self.ones, self.qT, self.wo_sb, self.bo

        self.x1d = nc.dram_tensor("x1_scratch", [T, D], F32,
                                  kind="Internal").ap()
        self.ln2d = nc.dram_tensor("ln2_scratch", [T, D], BF16,
                                   kind="Internal").ap()
        x1d = self.x1d
        ln2d = self.ln2d
        x3_g = nc.sbuf_tensor("x_t3", [128, 2 * D], F32, side="right")
        ao_g = nc.sbuf_tensor("ao", [128, 2 * D], F32, side="right")
        l23_g = nc.sbuf_tensor("l2p3", [128, 2 * D], BF16, side="right")
        bn3_g = nc.sbuf_tensor("bn3", [128, 2 * 32], F32, side="right")
        st3_g = nc.sbuf_tensor("st3", [128, 2 * 16], F32, side="right")
        wgs = [nc.psum_tensor(f"wo_ps{i}", [128, 512], F32) for i in range(4)]
        x3 = x3_g.__enter__().ap()
        ao = ao_g.__enter__().ap()
        l23 = l23_g.__enter__().ap()
        bn3 = bn3_g.__enter__().ap()
        st3 = st3_g.__enter__().ap()
        wops = [g.__enter__().ap() for g in wgs]
        for ti in range(TTN):
            pb = ti % 2
            aoa = ao[:, pb * D:(pb + 1) * D]
            xa = x3[:, pb * D:(pb + 1) * D]
            dma(sy, xa, io["x"][ti * 128:(ti + 1) * 128, :], tag=f"x3.{ti}")
            for oc in range(4):
                pp = wops[oc]
                for i in range(DT):
                    mm(pp, yT[:, i * T + ti * 128:i * T + (ti + 1) * 128],
                       wo_sb[:, oc * 8192 + i * 512:oc * 8192 + (i + 1) * 512],
                       start=(i == 0), stop=False, tag=f"wo{oc}.{ti}")
                mm(pp, ones[0:1, 0:128], bo[0:1, oc * 512:(oc + 1) * 512],
                   start=False, stop=True, tag=f"wob{oc}.{ti}")
                act(aoa[:, oc * 512:(oc + 1) * 512], pp, AF.Copy,
                    tag=f"woe{oc}.{ti}")
            bnb = bn3[:, pb * 32:pb * 32 + 32]
            agg = st3[:, pb * 16:pb * 16 + 2]
            veps = st3[:, pb * 16 + 2:pb * 16 + 3]
            rstd = st3[:, pb * 16 + 3:pb * 16 + 4]
            negmr = st3[:, pb * 16 + 4:pb * 16 + 5]
            self.ln_stats(aoa, bnb, agg, veps, rstd, negmr, tag=f"ln3.{ti}")
            act(aoa, aoa, AF.Identity, bias=negmr, scale=rstd, tag=f"ln3a.{ti}")
            tt(aoa, aoa, xa, ALU.add, tag=f"res3.{ti}")
            dma(sy, x1d[ti * 128:(ti + 1) * 128, :], aoa, tag=f"x1.{ti}")
            agg2 = st3[:, pb * 16 + 8:pb * 16 + 10]
            veps2 = st3[:, pb * 16 + 10:pb * 16 + 11]
            rstd2 = st3[:, pb * 16 + 11:pb * 16 + 12]
            negmr2 = st3[:, pb * 16 + 12:pb * 16 + 13]
            self.ln_stats(aoa, bnb, agg2, veps2, rstd2, negmr2,
                          tag=f"ln32.{ti}")
            l2a = l23[:, pb * D:(pb + 1) * D]
            act(l2a, aoa, AF.Identity, bias=negmr2, scale=rstd2,
                tag=f"ln32a.{ti}")
            dma(sy, ln2d[ti * 128:(ti + 1) * 128, :], l2a, tag=f"l2s.{ti}")
        for g_ in (st3_g, bn3_g, l23_g, ao_g, x3_g, *reversed(wgs)):
            g_.__exit__(None, None, None)
        self.qT_g.__exit__(None, None, None)
        self.bo_g.__exit__(None, None, None)
        self.wo_g.__exit__(None, None, None)
        if "x1" in dbg_t:
            with nc.sbuf_tensor("x1dbg", [128, D], F32) as xdg:
                xg = xdg.ap()
                for ti in range(TTN):
                    dma(sy, xg, x1d[ti * 128:(ti + 1) * 128, :])
                    dma(sy, dbg_t["x1"][ti * 128:(ti + 1) * 128, :], xg)

    def p4(self):
        nc, io, dbg_t = self.nc, self.io, self.dbg_t
        gp, sy = "gpsimd", "sync"
        mm, act, tt, ts, stt = self.mm, self.act, self.tt, self.ts, self.stt
        dma, transpose, vcopy = self.dma, self.transpose, self.vcopy
        ones = self.ones
        x1d = self.x1d

        wa_g = nc.sbuf_tensor("wa_sb", [128, 16 * 16], BF16)
        rb2_g = nc.sbuf_tensor("rb2_sb", [16, D], BF16)
        sb2a_g = nc.sbuf_tensor("sb2a", [1, D], BF16)
        sb2b_g = nc.sbuf_tensor("sb2b", [1, D], BF16)
        wa_sb = wa_g.__enter__().ap()
        rb2_sb = rb2_g.__enter__().ap()
        sb2a = sb2a_g.__enter__().ap()
        sb2b = sb2b_g.__enter__().ap()
        dma(gp, wa_sb, io["wa"].rearrange("p i e -> p (i e)"))
        dma(gp, rb2_sb, io["rb2"])
        dma(gp, sb2a, io["sb2"][0:1, :])
        dma(gp, sb2b, io["sb2"][1:2, :])

        acc_g = nc.sbuf_tensor("acc", [128, TTH * D], F32)
        ln2T_g = nc.sbuf_tensor("ln2T", [128, DT * TH], BF16)
        acc = acc_g.__enter__().ap()
        ln2T = ln2T_g.__enter__().ap()
        ln2T3 = ln2T.rearrange("p (j t) -> p j t", j=DT)

        names = ["x1_t", "ubuf", "bn4", "st4", "c_sb", "cbf", "cT",
                 "rsc", "wsl4", "ysc", "bgb", "bb1"]
        shapes = [([128, 2 * D], F32), ([128, 2 * D], F32),
                  ([128, 4 * 32], F32), ([128, 4 * 8], F32),
                  ([128, TTH * 16], F32), ([128, 16], BF16),
                  ([16, TH], BF16), ([128, 2 * 64], F32),
                  ([128, 2 * 8192], BF16), ([128, 8 * F], BF16),
                  ([128, 2 * 8], F32), ([128, 2 * 8], F32)]
        guards = [nc.sbuf_tensor(nm, sh, dt)
                  for nm, (sh, dt) in zip(names, shapes)]
        aps = [g.__enter__().ap() for g in guards]
        (x1t, ubuf, bn4, st4, c_sb, cbf, cTs, rsc, wsl4, ysc, bgb,
         bb1) = aps
        x1p = None
        l2t = ysc[:, 0:2 * D]
        self.ysc_full = ysc
        fgs = [nc.psum_tensor(f"f_ps{i}", [128, 512], F32) for i in range(6)]
        tgs = [nc.psum_tensor(f"tr4_ps{i}", [128, 1024], BF16)
               for i in range(2)]
        fps = [g.__enter__().ap() for g in fgs]
        trp4 = [g.__enter__().ap() for g in tgs]

        ctx = (x1p, l2t, bn4, st4, c_sb, cbf, cTs, rsc, fps, trp4, acc,
               ln2T, ln2T3, wa_sb)
        self.p4_prep1(0, ctx, interleave_epi=None)
        self.p4_prep23(0, ctx)
        if "ln2T" in dbg_t:
            for i in range(DT):
                dma(sy, dbg_t["ln2T"][i * 128:(i + 1) * 128, :],
                    ln2T[:, i * TH:(i + 1) * TH])
        self.p4_experts(0, wsl4, ysc, bgb, bb1, fps, acc, ln2T, c_sb,
                        cTs, rb2_sb, sb2a, sb2b)
        if "acc" in dbg_t:
            for tl in range(TTH):
                dma(sy, dbg_t["acc"][tl * 128:(tl + 1) * 128, :],
                    acc[:, tl * D:(tl + 1) * D])
        self.p4_prep1(1, ctx, interleave_epi=(x1t, ubuf, bn4, st4, acc))
        self.p4_prep23(1, ctx)
        self.p4_experts(1, wsl4, ysc, bgb, bb1, fps, acc, ln2T, c_sb,
                        cTs, rb2_sb, sb2a, sb2b)
        for tl in range(TTH):
            self.p4_epi_tile(1, tl, x1t, ubuf, bn4, st4, acc, fps)

        for g_ in (list(reversed(guards)) + [ln2T_g, acc_g, sb2b_g, sb2a_g,
                                             rb2_g, wa_g] +
                   list(reversed(tgs)) + list(reversed(fgs))):
            g_.__exit__(None, None, None)

    def p4_prep1(self, hh, ctx, interleave_epi=None):
        (x1p, l2t, bn4, st4, c_sb, cbf, cTs, rsc, fps, trp4, acc,
         ln2T, ln2T3, wa_sb) = ctx
        nc, io, dbg_t = self.nc, self.io, self.dbg_t
        sy = "sync"
        dma, transpose, vcopy = self.dma, self.transpose, self.vcopy
        ln2d = self.ln2d
        lslot = self.ysc_full
        for pf in range(2):
            ti0 = hh * TTH + pf
            dma("scalar", lslot[:, pf * D:(pf + 1) * D],
                ln2d[ti0 * 128:(ti0 + 1) * 128, :], tag=f"l2r{ti0}")
        for tl in range(TTH):
            if interleave_epi is not None:
                ex1t, eubuf, ebn4, est4, eacc = interleave_epi
                self.p4_epi_tile(hh - 1, tl, ex1t, eubuf, ebn4, est4, eacc,
                                 fps)
            ti = hh * TTH + tl
            if tl + 2 < TTH:
                tpf = hh * TTH + tl + 2
                dma("scalar", lslot[:, ((tl + 2) % 4) * D:((tl + 2) % 4 + 1) * D],
                    ln2d[tpf * 128:(tpf + 1) * 128, :], tag=f"l2r{tpf}")
            la = lslot[:, (tl % 4) * D:(tl % 4 + 1) * D]
            aslice = acc[:, tl * D:(tl + 1) * D]
            vcopy(aslice, la, tag=f"acci.{ti}")
            for half in range(2):
                pt = trp4[half]
                for jj in range(8):
                    j = half * 8 + jj
                    transpose(pt[:, jj * 128:(jj + 1) * 128],
                              la[:, j * 128:(j + 1) * 128], tag=f"tr4.{ti}.{j}")
                dst = ln2T3[:, half * 8:(half + 1) * 8, tl * 128:(tl + 1) * 128]
                src = pt.rearrange("p (j c) -> p j c", j=8)
                vcopy(dst, src, tag=f"trc4.{ti}.{half}")

    def p4_prep23(self, hh, ctx):
        (x1p, l2t, bn4, st4, c_sb, cbf, cTs, rsc, fps, trp4, acc,
         ln2T, ln2T3, wa_sb) = ctx
        nc, io, dbg_t = self.nc, self.io, self.dbg_t
        sy = "sync"
        mm, act, tt, ts, stt = self.mm, self.act, self.tt, self.ts, self.stt
        dma, transpose, vcopy = self.dma, self.transpose, self.vcopy
        for tl in range(TTH):
            ti = hh * TTH + tl
            pa = fps[tl % 2]
            for i in range(DT):
                mm(pa[:, 0:16],
                   ln2T[:, i * TH + tl * 128:i * TH + (tl + 1) * 128],
                   wa_sb[:, i * 16:(i + 1) * 16], start=(i == 0),
                   stop=(i == DT - 1), tag=f"aff{ti}")
            afa = c_sb[:, tl * 16:(tl + 1) * 16]
            act(afa, pa[:, 0:16], AF.Exp, scale=-1.0, tag=f"affs{ti}")
            ts(afa, afa, 1.0, None, ALU.add, tag=f"affs{ti}")
            self.vrecip(afa, afa, tag=f"affs{ti}")
            if "aff" in dbg_t:
                dma(sy, dbg_t["aff"][ti * 128:(ti + 1) * 128, :], afa)
        for tl in range(TTH):
            ti = hh * TTH + tl
            pb = tl % 2
            afa = c_sb[:, tl * 16:(tl + 1) * 16]
            m1 = st4[:, pb * 8 + 5:pb * 8 + 6]
            m2 = st4[:, pb * 8 + 6:pb * 8 + 7]
            rden = st4[:, pb * 8 + 7:pb * 8 + 8]
            msk1 = rsc[:, pb * 64:pb * 64 + 16]
            aff2 = rsc[:, pb * 64 + 16:pb * 64 + 32]
            msk2 = rsc[:, pb * 64 + 32:pb * 64 + 48]
            msum = rsc[:, pb * 64 + 48:pb * 64 + 64]
            self.vop(lambda e, o=m1, i=afa: e.reduce_max(o, i, AXX),
                     [afa], [m1], tag=f"rt{ti}")
            ts(msk1, afa, m1, None, ALU.is_equal, tag=f"rt{ti}")
            stt(aff2, msk1, NEG, afa, ALU.mult, ALU.add, tag=f"rt{ti}")
            self.vop(lambda e, o=m2, i=aff2: e.reduce_max(o, i, AXX),
                     [aff2], [m2], tag=f"rt{ti}")
            ts(msk2, aff2, m2, None, ALU.is_equal, tag=f"rt{ti}")
            tt(msum, msk1, msk2, ALU.add, tag=f"rt{ti}")
            tt(m1, m1, m2, ALU.add, tag=f"rt{ti}")
            self.vrecip(rden, m1, tag=f"rt{ti}")
            stt(afa, afa, rden, msum, ALU.mult, ALU.mult, tag=f"rt{ti}")
            if "c" in dbg_t:
                dma(sy, dbg_t["c"][ti * 128:(ti + 1) * 128, :], afa)
            vcopy(cbf, afa, tag=f"cb{ti}")
            ctp = trp4[tl % 2]
            transpose(ctp[0:16, 0:128], cbf, tag=f"ctr{ti}")
            vcopy(cTs[:, tl * 128:(tl + 1) * 128], ctp[0:16, 0:128],
                  tag=f"ctc{ti}")

    def p4_experts(self, hh, wsl4, ysc, bgb, bb1, fps, acc, ln2T, c_sb,
                   cTs, rb2_sb, sb2a, sb2b):
        io = self.io
        gp, sy = "gpsimd", "sync"
        mm, act, tt, stt = self.mm, self.act, self.tt, self.stt
        dma = self.dma
        ones = self.ones
        par = 0
        for e in range(NE + NSH):
            routed = e < NE
            ei = e if routed else e - NE
            wsrc = (io["rg"], io["rw1"], io["rw2"]) if routed else \
                   (io["sg"], io["sw1"], io["sw2"])
            bsrc = (io["rgb"], io["rb1"]) if routed else (io["sgb"], io["sb1"])
            ebp = e % 2
            dma(sy, bgb[:, ebp * 8:ebp * 8 + 8], bsrc[0][ei], tag=f"bgb{e}")
            dma(sy, bb1[:, ebp * 8:ebp * 8 + 8], bsrc[1][ei], tag=f"bb1{e}")
            for fh in range(2):
                w = wsl4[:, (par % 2) * 8192:(par % 2) * 8192 + 8192]
                dma(gp, w, wsrc[0][ei, fh].rearrange("p i f -> p (i f)"),
                    tag=f"g{e}.{fh}")
                par += 1
                for fc in range(4):
                    fg = fh * 4 + fc
                    for tc in range(2):
                        pp = fps[2 + (fg * 2 + tc) % 2]
                        for i in range(DT):
                            mm(pp,
                               w[:, i * 512 + fc * 128:i * 512 + (fc + 1) * 128],
                               ln2T[:, i * TH + tc * 512:i * TH + (tc + 1) * 512],
                               start=(i == 0), stop=(i == DT - 1),
                               tag=f"gm{e}.{fg}.{tc}")
                        gb_ap = bgb[:, ebp * 8 + fg:ebp * 8 + fg + 1]
                        act(ysc[:, fg * F + tc * 512:fg * F + (tc + 1) * 512],
                            pp, AF.Gelu, bias=gb_ap, tag=f"ge{e}.{fg}.{tc}")
            for fh in range(2):
                w = wsl4[:, (par % 2) * 8192:(par % 2) * 8192 + 8192]
                dma(gp, w, wsrc[1][ei, fh].rearrange("p i f -> p (i f)"),
                    tag=f"w1{e}.{fh}")
                par += 1
                for fc in range(4):
                    fg = fh * 4 + fc
                    for tc in range(2):
                        pp = fps[2 + (fg * 2 + tc) % 2]
                        for i in range(DT):
                            mm(pp,
                               w[:, i * 512 + fc * 128:i * 512 + (fc + 1) * 128],
                               ln2T[:, i * TH + tc * 512:i * TH + (tc + 1) * 512],
                               start=(i == 0), stop=(i == DT - 1),
                               tag=f"w1m{e}.{fg}.{tc}")
                        b1_ap = bb1[:, ebp * 8 + fg:ebp * 8 + fg + 1]
                        ya = ysc[:, fg * F + tc * 512:fg * F + (tc + 1) * 512]
                        stt(ya, pp, b1_ap, ya, ALU.add, ALU.mult,
                            tag=f"w1e{e}.{fg}.{tc}")
            for fh in range(2):
                w = wsl4[:, (par % 2) * 8192:(par % 2) * 8192 + 8192]
                dma(gp, w, wsrc[2][ei, fh].rearrange("p j d -> p (j d)"),
                    tag=f"w2{e}.{fh}")
                par += 1
                special = (e == NE and fh == 0)
                special2 = (e == NE + 1 and fh == 0)
                for tl in range(TTH):
                    for dc in range(4):
                        pp = fps[2 + (tl * 4 + dc) % 4]
                        for jj in range(4):
                            fg = fh * 4 + jj
                            mm(pp,
                               ysc[:, fg * F + tl * 128:fg * F + (tl + 1) * 128],
                               w[:, jj * D + dc * 512:jj * D + (dc + 1) * 512],
                               start=(jj == 0),
                               stop=(jj == 3 and not (special or special2)),
                               tag=f"w2m{e}.{fh}.{tl}.{dc}")
                        if special:
                            mm(pp, ones[0:1, 0:128],
                               sb2a[0:1, dc * 512:(dc + 1) * 512],
                               start=False, stop=False, tag=f"sb2a.{tl}.{dc}")
                            mm(pp, cTs[:, tl * 128:(tl + 1) * 128],
                               rb2_sb[:, dc * 512:(dc + 1) * 512],
                               start=False, stop=True, tag=f"rb2.{tl}.{dc}")
                        if special2:
                            mm(pp, ones[0:1, 0:128],
                               sb2b[0:1, dc * 512:(dc + 1) * 512],
                               start=False, stop=True, tag=f"sb2b.{tl}.{dc}")
                        aslice = acc[:, tl * D + dc * 512:tl * D + (dc + 1) * 512]
                        if routed:
                            ce = c_sb[:, tl * 16 + ei:tl * 16 + ei + 1]
                            stt(aslice, pp, ce, aslice, ALU.mult, ALU.add,
                                tag=f"w2e{e}.{fh}.{tl}.{dc}")
                        else:
                            tt(aslice, aslice, pp, ALU.add,
                               tag=f"w2e{e}.{fh}.{tl}.{dc}")


    def p4_epi_tile(self, hh, tl, x1t, ubuf, bn4, st4, acc, fps):
        sy = "sync"
        act, dma = self.act, self.dma
        x1d, out = self.x1d, self.out
        ti = hh * TTH + tl
        pb = tl % 2
        xa = x1t[:, pb * D:(pb + 1) * D]
        ep = ubuf[:, pb * D:(pb + 1) * D]
        dma(sy, xa, x1d[ti * 128:(ti + 1) * 128, :], tag=f"x1e{ti}")
        aslice = acc[:, tl * D:(tl + 1) * D]
        parts = bn4[:, pb * 32:pb * 32 + 10]
        veps = st4[:, pb * 8 + 2:pb * 8 + 3]
        rstd = st4[:, pb * 8 + 3:pb * 8 + 4]
        negmr = st4[:, pb * 8 + 4:pb * 8 + 5]
        self.ln_stats_scalar(aslice, (fps[4], fps[5]), parts, veps, rstd,
                             negmr, tag=f"lnE.{ti}")
        act(ep, aslice, AF.Identity, bias=negmr, scale=rstd, tag=f"lnEa.{ti}")
        self.S.op("gpsimd",
                  lambda e, o=ep, a=ep, b=xa: e.tensor_tensor(o, a, b, ALU.add),
                  reads=[ep, xa], writes=[ep], tag=f"resE.{ti}")
        dma(sy, out[ti * 128:(ti + 1) * 128, :], ep, tag=f"out{ti}")


def build_nc(dbg=(), stop_after=None):
    nc = bass.Bass("TRN2", target_bir_lowering=False, debug=False)
    io = {}
    for name, shape, dt in INPUT_SPECS:
        io[name] = nc.dram_tensor(name, shape, dt, kind="ExternalInput").ap()
    out = nc.dram_tensor("out", [T, D], F32, kind="ExternalOutput").ap()
    dbg_t = {}
    for name in dbg:
        shp, dt = DBG_SPECS[name]
        dbg_t[name] = nc.dram_tensor("dbg_" + name, shp, dt,
                                     kind="ExternalOutput").ap()
    b = Builder(nc, io, out, dbg_t)
    S = b.build(stop_after=stop_after)
    return nc, S


from concourse.bass_utils import run_bass_kernel_spmd


def kernel(**inputs):
    shared = prep_shared(inputs)
    nc, _ = build_nc()
    in_maps = []
    for b in range(8):
        m = dict(shared)
        m["x"] = np.ascontiguousarray(inputs["L0_attn_x"][b])
        in_maps.append(m)
    res = run_bass_kernel_spmd(nc, in_maps, core_ids=list(range(8)))
    return np.stack([r["out"] for r in res.results], axis=0)
